# version 46 (final)
# baseline (speedup 1.0000x reference)
import contextlib
import numpy as np
import ml_dtypes

import concourse.bass as bass
import concourse.mybir as mybir
from concourse.bass_utils import run_bass_kernel_spmd

F32 = mybir.dt.float32
BF16 = mybir.dt.bfloat16
I32 = mybir.dt.int32
AF = mybir.ActivationFunctionType
ALU = mybir.AluOpType
AXX = mybir.AxisListType.X

NCORES = 8
P = 128
DM = 4096
L = 8192
DIC = 1024
NH = 16
HD = 64
TC = 512
NTC = L // TC
CH = 128
NCHTC = TC // CH
HF = 256
WA1C = DIC + NH
WA2C = DIC + 256
NIC = DM // P
RG = [list(range(NCORES))]
NEG = -1e30


class Sem:
    def __init__(self, ctx, nc, name):
        self.h = ctx.enter_context(nc.semaphore(name))
        self.n = 0


class KB:

    def __init__(self, debug=False):
        self.debug = debug
        nc = bass.Bass("TRN2", target_bir_lowering=False, debug=False)
        self.nc = nc
        self.ops = {k: [] for k in ("sync", "tensor", "vector", "scalar", "gpsimd")}
        di = {}

        def ein(name, shape, dt=F32):
            di[name] = nc.dram_tensor(name, shape, dt, kind="ExternalInput").ap()

        ein("x_own", [1024, DM])
        ein("wa1t", [DM, WA1C])
        ein("wa2t", [DM, WA2C])
        ein("wot", [DIC, DM])
        ein("cw_pg", [P, 40])
        ein("cb_pg", [P, 10])
        ein("dtb", [P, 1])
        ein("alog", [P, 1])
        ein("dvec_row", [1, NH])
        ein("normw_pg", [P, 8])
        ein("rwt", [DM, 64])
        ein("rbias", [1, 64])
        ein("eb", [64, DM])
        ein("identf", [P, P])
        ein("identb", [P, P], BF16)
        ein("onesrow_f", [1, P])
        ein("onescol_f", [P, 1])
        ein("trimask", [P, P], BF16)
        ein("tok0", [1, 1], I32)
        self.di = di
        self.out = nc.dram_tensor("out", [1024, DM], F32, kind="ExternalOutput").ap()

        def idr(name, shape, dt=F32):
            return nc.dram_tensor(name, shape, dt, kind="Internal").ap()

        self.lnxt_in = [idr(f"lnxt_in{h}", [DM, TC], BF16) for h in range(2)]
        self.lnxt_all = [idr(f"lnxt_all{h}", [NCORES * DM, TC], BF16)
                         for h in range(2)]
        self.wout_in = [idr(f"wout_in{q}", [DIC, 1024], BF16) for q in range(4)]
        self.wout_all = [idr(f"wout_all{q}", [NCORES * DIC, 1024], BF16)
                         for q in range(4)]
        self.z_spill = idr("z_spill", [DIC, L], BF16)
        self.dt_spill = idr("dt_spill", [NH, L])
        self.y_blk = idr("y_blk", [NTC * DIC, TC], BF16)
        self.y_allb = idr("y_allb", [NTC * NCORES * DIC, TC], BF16)
        self.ssq_in = idr("ssq_in", [1, L])
        self.ssq_out = idr("ssq_out", [1, L])
        self.combo_dr = idr("combo_dr", [NH, TC])
        self.combo_drb = idr("combo_drb", [2 * NH, TC], BF16)
        self.wa2bf = idr("wa2bf", [DM, WA2C], BF16)
        self.out_spill = idr("out_spill", [1024, DM])
        self.dbgout = {}
        if debug:
            for name, shape, dt in [
                ("d_lnxt", [NCORES * DM, 1024], BF16),
                ("d_stats", [P, 16], F32),
                ("d_x7", [P, DM], F32),
                ("d_st2", [P, 8], F32),
                ("d_wbuf", [P, NIC * WA1C], BF16),
                ("d_combo", [P, TC], F32),
                ("d_ctok", [P, TC], F32),
                ("d_xdt", [P, 4096], BF16),
                ("d_xdtd", [P, 4096], BF16),
                ("d_ytok", [P, 4096], BF16),
                ("d_sbf", [P, 4096], BF16),
                ("d_ln7", [P, DM], BF16),
                ("d_z", [DIC, L], BF16),
                ("d_dt", [NH, L], F32),
                ("d_y", [NTC * DIC, TC], BF16),
                ("d_ssq", [1, L], F32),
                ("d_osp", [1024, DM], F32),
            ]:
                self.dbgout[name] = nc.dram_tensor(name, shape, dt,
                                                   kind="ExternalOutput").ap()

    def op(self, engine, fn, waits=(), inc=None, amt=1):
        if inc is not None:
            inc.n += amt

        def g(eng, fn=fn, waits=tuple(waits), inc=inc, amt=amt):
            for (s, v) in waits:
                if v > 0:
                    eng.wait_ge(s.h, v)
            ins = fn(eng)
            if inc is not None:
                ins.then_inc(inc.h, amt)
        self.ops[engine].append(g)

    def build(self):
        nc = self.nc
        with contextlib.ExitStack() as ctx:
            self.alloc(ctx)
            self.emit_all()
            blk = ctx.enter_context(nc.Block())
            blk.sync(lambda e: [f(e) for f in self.ops["sync"]])
            blk.tensor(lambda e: [f(e) for f in self.ops["tensor"]])
            blk.vector(lambda e: [f(e) for f in self.ops["vector"]])
            blk.scalar(lambda e: [f(e) for f in self.ops["scalar"]])
            blk.gpsimd(lambda e: [f(e) for f in self.ops["gpsimd"]])
        return nc

    def alloc(self, ctx):
        nc = self.nc
        sb = lambda name, shape, dt=F32: ctx.enter_context(nc.sbuf_tensor(name, shape, dt))
        ps = lambda name, shape, dt=F32: ctx.enter_context(nc.psum_tensor(name, shape, dt))
        self.wbuf = sb("wbuf", [P, NIC * WA2C], BF16)
        self.lnx = sb("lnx", [P, 16384], BF16)
        self.wstage = sb("wstage", [P, 2560], F32)
        self.arena = sb("arena", [P, 24576], BF16)
        a = self.arena.ap()
        self.xbcpre = a[:, 0:5150]
        self.xbcpost = a[:, 5152:10272]
        self.siluz = a[:, 10272:14368]
        self.yfm = a[:, 14368:18464]
        self.xdt = a[:, 18464:19488]
        self.xdtd = a[:, 19488:20512]
        self.ytok = a[:, 20512:21536]
        self.zstg = a[:, 21536:22048]
        self.xD = a[:, 22528:24576].bitcast(F32)
        self.a0_x = a[:, 0:8192].bitcast(F32)
        self.a0_ln = a[:, 8192:12288]
        self.a0_tr = a[:, 12288:13312]
        self.a0_wo = a[:, 13312:15360]
        self.a0_sq = a[:, 8192:12288].bitcast(F32)
        self.a0_xb = a[:, 15360:23552].bitcast(F32)
        self.lnxb0 = a[:, 0:8192]
        self.lnxb1 = a[:, 15360:23552]
        self.wa2st = a[:, 8192:10752]
        pass
        self.c_x = a[:, 0:8192].bitcast(F32)
        self.c_out = a[:, 8192:16384].bitcast(F32)
        self.c_ln2f = self.c_out
        self.c_t1 = a[:, 16384:24576].bitcast(F32)
        self.c_ln2b = a[:, 16384:20480]
        self.c_ln2t = a[:, 20480:24576]
        self.c_dump = a[:, 16384:20480]
        self.c_x2 = self.c_x
        self.c_outst = self.c_x
        wb = self.wbuf.ap()
        self.b_y = wb[:, 0:32768]
        self.c_eb = wb[:, 32768:36864]
        self.c_rwt = wb[:, 36864:38912]
        self.c_stg = wb[:, 38912:40960].bitcast(F32)
        ln = self.lnx.ap()
        self.b_w = ln[:, 0:16384]
        self.c_sq = ln[:, 0:8192].bitcast(F32)
        self.identf = sb("identf_s", [P, P])
        self.identb = sb("identb_s", [P, P], BF16)
        self.onesrow = sb("onesrow_s", [1, P])
        self.onescol = sb("onescol_s", [P, 1])
        self.trimask = sb("trimask_s", [P, P], BF16)
        self.cw = sb("cw_s", [P, 40])
        self.cb = sb("cb_s", [P, 10])
        self.dtbias = sb("dtb_s", [P, 1])
        self.alogc = sb("alog_s", [P, 1])
        self.apos = sb("apos_s", [P, 1])
        self.dvrow = sb("dvrow_s", [1, NH])
        self.dbc = sb("dbc_s", [P, NH])
        self.normw = sb("normw_s", [P, 8])
        self.dgw = sb("dgw_s", [P, 40 * P], BF16)
        self.mcol = sb("mcol_s", [P, 8])
        self.invcol = sb("invcol_s", [P, 8])
        self.irms = sb("irms_s", [P, 8])
        self.statsc = sb("statsc_s", [P, 8])
        self.sp2 = sb("sp2_s", [NH, TC])
        self.scanmask = sb("scanmask_s", [NH, TC])
        self.combo = sb("combo_s", [P, TC])
        self.ctok = sb("ctok_s", [P, TC])
        self.atok = sb("atok_s", [P, NH])
        self.dstok = sb("dstok_s", [P, NH])
        self.dsdiff = sb("dsdiff_s", [P, NH])
        self.expTb = sb("expTb_s", [P, NH])
        self.S_in = sb("S_in_s", [P, 1024])
        self.S_bf = sb("S_bf_s", [P, 1024], BF16)
        self.S_a = sb("S_a_s", [P, 1024])
        self.dtr = self.S_a.ap()[0:NH, 0:TC]
        self.Ebuf = sb("Ebuf_s", [P, 4 * P], BF16)
        self.Mbuf = sb("Mbuf_s", [P, 4 * P], BF16)
        self.Gm = sb("Gm_s", [P, P], BF16)
        self.Btok = sb("Btok_s", [P, P], BF16)
        self.yscr = sb("yscr_s", [P, 512])
        self.sp1 = self.yscr.ap()[0:NH, :]
        self.cumA = self.sp2.ap()[:]
        self.cumB = self.sp1
        self.sqs = self.S_a
        self.sqs_bf = self.S_a.ap()[:, 0:512].bitcast(BF16)
        self.ssqrow8 = self.S_a
        self.flatr = self.wstage
        self.b_sb2 = [self.wstage.ap()[:, 0:512],
                      self.wstage.ap()[:, 512:1024]]
        self.b_tt2 = [self.wstage.ap()[:, 1024:1536],
                      self.wstage.ap()[:, 1536:2048]]
        self.fbf = self.wstage.ap()[0:1, 0:2048].bitcast(BF16)
        self.chi = a[0:NH, 20512:21024]
        self.clo = a[0:NH, 21024:21536]
        self.zevac = None
        self.csc = sb("csc_s", [P, 512])
        self.zevac = self.sqs.ap()[:, 0:256].bitcast(BF16)
        self.zevac2 = self.sqs.ap()[:, 256:512].bitcast(BF16)
        self.dtev = self.csc.ap()[0:NH, :]
        self.ssqr = self.csc
        self.csc2 = sb("csc2_s", [P, 8])
        self.ubf = sb("ubf_s", [P, 64], BF16)
        self.ubT = sb("ubT_s", [64, P], BF16)
        self.rbias_b = sb("rbias_bs", [1, 64], BF16)
        self.wsum = sb("wsum_s", [P, 1])
        self.epsb = sb("epsb_s", [P, 1])
        self.pb = [ps(f"pb{i}", [P, 512]) for i in range(6)]
        self.pb6 = ps("pb6", [P, 1024], BF16)
        self.pb7 = ps("pb7", [P, 512])
        self.CC = Sem(ctx, nc, "CC")
        self.PE = Sem(ctx, nc, "PE")
        self.DV = Sem(ctx, nc, "DV")
        self.AC = Sem(ctx, nc, "AC")
        names = ["setup", "x", "wst0", "wst1", "lnx0", "lnx1", "dtr",
                 "z0", "z1", "bw0", "bw1", "ceb", "cx",
                 "gl", "wo0", "wo1", "zs0", "zs1", "ds0", "ds1",
                 "cm", "fl", "sq", "yi", "cs", "os", "out", "dbg", "by",
                 "wa2", "wa2l", "chl"]
        self.S = {n: Sem(ctx, nc, "S_" + n) for n in names}
        self.t = {}

    def g(self, key, default=0):
        return self.t.get(key, default)

    def emit_all(self):
        self.emit_setup()
        self.emit_a0()
        self.emit_a1()
        self.emit_a2()
        self.emit_b()
        self.emit_c_rest()

    def emit_setup(self):
        di, AC, DV, PE = self.di, self.AC, self.DV, self.PE
        SET = self.S["setup"]
        loads = [
            (self.identf, "identf"), (self.identb, "identb"),
            (self.onesrow, "onesrow_f"), (self.onescol, "onescol_f"),
            (self.trimask, "trimask"), (self.cw, "cw_pg"), (self.cb, "cb_pg"),
            (self.dtbias, "dtb"), (self.alogc, "alog"), (self.dvrow, "dvec_row"),
            (self.normw, "normw_pg"),
        ]
        for buf, name in loads:
            self.op("sync", lambda e, b=buf, n=name: e.dma_start(out=b[:], in_=di[n]),
                    inc=SET, amt=16)
        self.ld_setup = SET.n
        self.SET = SET
        self.op("vector", lambda e: e.memset(self.epsb[:], 1e-5), inc=DV)
        self.dv_eps = DV.n
        self.op("vector", lambda e: e.memset(self.scanmask.ap()[:], 1.0), inc=DV)
        self.op("vector", lambda e: e.memset(
            self.scanmask.ap().rearrange("h (c t) -> h c t", t=CH)[:, :, 0:1],
            0.0), waits=[(DV, DV.n)], inc=DV)
        self.op("scalar", lambda e: e.activation(out=self.apos[:], in_=self.alogc[:],
                                                 func=AF.Exp),
                waits=[(self.SET, self.ld_setup)], inc=AC)
        self.ac_apos = AC.n
        self.op("tensor", lambda e: e.matmul(
            out=self.pb[0].ap()[:, 0:NH], lhsT=self.onesrow[:], rhs=self.dvrow[:],
            start=True, stop=True), waits=[(self.SET, self.ld_setup)], inc=PE)
        pe_d = PE.n
        self.op("vector", lambda e: e.tensor_copy(out=self.dbc[:],
                                                  in_=self.pb[0].ap()[:, 0:NH]),
                waits=[(PE, pe_d)], inc=DV)
        self.dv_dbc = DV.n
        for g in range(10):
            for k in range(4):
                self.op("vector", lambda e, g=g, k=k: e.tensor_scalar(
                    out=self.dgw.ap()[:, (g * 4 + k) * P:(g * 4 + k + 1) * P],
                    in0=self.identb[:], scalar1=self.cw[:, g * 4 + k:g * 4 + k + 1],
                    scalar2=None, op0=ALU.mult),
                    waits=([(self.SET, self.ld_setup)] if g == 0 and k == 0 else []),
                    inc=(DV if (g == 9 and k == 3) else None))
        self.dv_diag = DV.n

    def emit_a0(self):
        di = self.di
        CC, PE, DV, AC = self.CC, self.PE, self.DV, self.AC
        SX, SGL, SDBG = self.S["x"], self.S["gl"], self.S["dbg"]
        self.wa1_ready = []
        for j in range(8):
            xb = self.a0_x if j % 2 == 0 else self.a0_xb
            self.op("sync", lambda e, j=j, xb=xb: e.dma_start(
                out=xb, in_=di["x_own"][j * 128:(j + 1) * 128, :]),
                waits=[(DV, self.g(f"a0_xfree_{j % 2}"))], inc=SX, amt=16)
            ld_x = SX.n
            self.op("scalar", lambda e, xb=xb: e.activation(
                out=self.a0_ln, in_=xb, func=AF.Copy,
                accum_out=self.statsc[:, 0:1]),
                waits=[(SX, ld_x), (DV, self.g("a0_stats_done")),
                       (PE, self.g("a0_trpe")), (AC, AC.n)], inc=AC)
            self.op("scalar", lambda e, xb=xb: e.activation(
                out=self.a0_ln, in_=xb, func=AF.Square,
                accum_out=self.statsc[:, 2:3]), waits=[(AC, AC.n)], inc=AC)
            ac_acc = AC.n
            self.op("vector", lambda e, j=j: e.tensor_scalar(
                out=self.mcol[:, j:j + 1], in0=self.statsc[:, 0:1],
                scalar1=1.0 / DM, scalar2=None, op0=ALU.mult),
                waits=[(AC, ac_acc)], inc=DV)
            self.op("vector", lambda e: e.tensor_scalar(
                out=self.statsc[:, 3:4], in0=self.statsc[:, 2:3],
                scalar1=1.0 / DM, scalar2=None, op0=ALU.mult),
                waits=[(DV, DV.n)], inc=DV)
            self.op("vector", lambda e, j=j: e.tensor_tensor(
                out=self.statsc[:, 4:5], in0=self.mcol[:, j:j + 1],
                in1=self.mcol[:, j:j + 1], op=ALU.mult),
                waits=[(DV, DV.n)], inc=DV)
            self.op("vector", lambda e: e.tensor_tensor(
                out=self.statsc[:, 5:6], in0=self.statsc[:, 3:4],
                in1=self.statsc[:, 4:5], op=ALU.subtract),
                waits=[(DV, DV.n)], inc=DV)
            dv_var = DV.n
            self.t["a0_stats_done"] = dv_var
            self.op("scalar", lambda e: e.activation(
                out=self.statsc[:, 6:7], in_=self.statsc[:, 5:6],
                func=AF.Sqrt, bias=self.epsb[:]),
                waits=[(DV, dv_var), (DV, self.dv_eps)], inc=AC)
            ac_sd = AC.n
            self.op("vector", lambda e, j=j: e.reciprocal(
                out=self.invcol[:, j:j + 1], in_=self.statsc[:, 6:7]),
                waits=[(AC, ac_sd)], inc=DV)
            dv_inv = DV.n
            self.op("vector", lambda e, j=j, xb=xb: e.tensor_scalar(
                out=self.a0_ln, in0=xb,
                scalar1=self.mcol[:, j:j + 1], scalar2=self.invcol[:, j:j + 1],
                op0=ALU.subtract, op1=ALU.mult),
                waits=[(DV, dv_inv), (PE, self.g("a0_trpe"))], inc=DV)
            dv_ln = DV.n
            self.t[f"a0_xfree_{j % 2}"] = dv_ln
            for b in range(8):
                half = b % 2
                for t in range(4):
                    fc = b * 4 + t
                    self.op("tensor", lambda e, fc=fc, t=t, half=half: e.transpose(
                        out=self.pb6.ap()[:, half * 512 + t * 128:
                                          half * 512 + (t + 1) * 128],
                        in_=self.a0_ln[:, fc * 128:(fc + 1) * 128],
                        identity=self.identb[:]),
                        waits=([(DV, dv_ln), (DV, self.g("a0_bf0")),
                                (DV, self.g("a0_bf1"))]
                               if t == 0 else []),
                        inc=(PE if t == 3 else None))
                pe_tr = PE.n
                self.t["a0_trpe"] = pe_tr
                self.op("vector", lambda e, half=half: e.tensor_copy(
                    out=self.a0_tr[:, half * 512:(half + 1) * 512],
                    in_=self.pb6.ap()[:, half * 512:(half + 1) * 512]),
                    waits=[(PE, pe_tr), (SGL, self.g("a0_trsp"))], inc=DV)
                dv_ev = DV.n
                self.t[f"a0_bf{half}"] = dv_ev
                self.op("gpsimd", lambda e, j=j, b=b, half=half: e.dma_start(
                    out=self.lnxt_in[j // 4][b * 512:(b + 1) * 512,
                                             (j % 4) * 128:(j % 4 + 1) * 128
                                             ].rearrange("(a p) t -> p a t", p=128),
                    in_=self.a0_tr[:, half * 512:(half + 1) * 512].rearrange(
                        "p (a t) -> p a t", a=4)),
                    waits=[(DV, dv_ev)], inc=SGL, amt=16)
                self.t["a0_trsp"] = SGL.n
            if j == 3 or j == 7:
                hh = j // 4
                self.op("gpsimd", lambda e, hh=hh: e.collective_compute(
                    "AllGather", ALU.bypass, replica_groups=RG,
                    ins=[self.lnxt_in[hh].opt()], outs=[self.lnxt_all[hh].opt()]),
                    waits=[(SGL, SGL.n)], inc=CC)
                if hh == 0:
                    self.cc_lnxt0 = CC.n
                else:
                    self.cc_lnxt1 = CC.n
            for ic in range(4 * j, 4 * j + 4):
                self.emit_wa1_piece(ic)
        if self.debug:
            self.op("gpsimd", lambda e: e.dma_start(
                out=self.dbgout["d_x7"], in_=self.a0_xb),
                waits=[(DV, DV.n)], inc=SDBG, amt=16)
            self.op("gpsimd", lambda e: e.dma_start(
                out=self.dbgout["d_ln7"], in_=self.a0_ln), inc=SDBG, amt=16)
            self.op("gpsimd", lambda e: e.dma_start(
                out=self.dbgout["d_stats"][:, 0:8], in_=self.mcol[:]), inc=SDBG, amt=16)
            self.op("gpsimd", lambda e: e.dma_start(
                out=self.dbgout["d_stats"][:, 8:16], in_=self.invcol[:]),
                inc=SDBG, amt=16)
            self.op("gpsimd", lambda e: e.dma_start(
                out=self.dbgout["d_st2"], in_=self.statsc[:]), inc=SDBG, amt=16)
        self.sp_lnxt = SGL.n
        self.a0_last_sq_dv = DV.n

    def emit_wconv(self, wname, ncols, first_waits):
        di, DV = self.di, self.DV
        ready = []
        for ic in range(NIC):
            slot = ic % 2
            SW = self.S[f"wst{slot}"]
            ws = self.wstage.ap()[:, slot * 1280: slot * 1280 + ncols]
            w = [(DV, self.g(f"ws_free_{slot}"))]
            if ic < 2:
                w += first_waits
            self.op("sync", lambda e, ic=ic, ws=ws: e.dma_start(
                out=ws, in_=di[wname][ic * 128:(ic + 1) * 128, :]),
                waits=w, inc=SW, amt=16)
            ld_w = SW.n
            dst = self.wbuf.ap()[:, ic * ncols:(ic + 1) * ncols]
            self.op("vector", lambda e, ws=ws, dst=dst: e.tensor_copy(
                out=dst, in_=ws), waits=[(SW, ld_w)] + (first_waits if ic < 2 else []),
                inc=DV)
            self.t[f"ws_free_{slot}"] = DV.n
            ready.append(DV.n)
        return ready

    def emit_wa1_piece(self, ic):
        di, AC = self.di, self.AC
        slot = ic % 2
        SW = self.S[f"wst{slot}"]
        ws = self.wstage.ap()[:, slot * 1280: slot * 1280 + WA1C]
        self.op("sync", lambda e, ws=ws, ic=ic: e.dma_start(
            out=ws, in_=di["wa1t"][ic * 128:(ic + 1) * 128, :]),
            waits=[(AC, self.g(f"wsA_free_{slot}"))], inc=SW, amt=16)
        ld_w = SW.n
        dst = self.wbuf.ap()[:, ic * WA1C:(ic + 1) * WA1C]
        self.op("scalar", lambda e, ws=ws, dst=dst: e.activation(
            out=dst, in_=ws, func=AF.Copy), waits=[(SW, ld_w)], inc=AC)
        self.t[f"wsA_free_{slot}"] = AC.n
        self.wa1_ready.append(AC.n)

    def emit_wout_piece(self, oc, hh, slot, extra_waits=()):
        di, DV = self.di, self.DV
        SW = self.S[f"wst{slot}"]
        SWO = self.S[f"wo{slot}"]
        ws = self.wstage.ap()[:, slot * 1280:slot * 1280 + 1024]
        self.op("sync", lambda e, oc=oc, hh=hh, ws=ws: e.dma_start(
            out=ws, in_=di["wot"][oc * 128:(oc + 1) * 128,
                                  hh * 1024:(hh + 1) * 1024]),
            waits=[(DV, self.g(f"ws_free_{slot}"))] + list(extra_waits),
            inc=SW, amt=16)
        ld_w = SW.n
        ob = self.a0_wo[:, slot * 1024:(slot + 1) * 1024]
        self.op("vector", lambda e, ws=ws, ob=ob, oc=oc: e.tensor_scalar(
            out=ob, in0=ws, scalar1=self.normw[:, oc:oc + 1],
            scalar2=None, op0=ALU.mult),
            waits=[(SW, ld_w), (SWO, self.g(f"wo_sp_{slot}"))], inc=DV)
        dv_w = DV.n
        self.t[f"ws_free_{slot}"] = dv_w
        self.op("gpsimd", lambda e, oc=oc, hh=hh, ob=ob: e.dma_start(
            out=self.wout_in[hh][oc * 128:(oc + 1) * 128, :], in_=ob),
            waits=[(DV, dv_w)], inc=SWO, amt=16)
        self.t[f"wo_sp_{slot}"] = SWO.n

    def emit_wa2_piece(self, ic):
        di, DV = self.di, self.DV
        slot = ic % 2
        SW = self.S[f"wst{slot}"]
        SWA = self.S["wa2"] if slot == 0 else self.S["cm"]
        ws = self.wstage.ap()[:, slot * 1280:slot * 1280 + WA2C]
        self.op("sync", lambda e, ic=ic, ws=ws: e.dma_start(
            out=ws, in_=di["wa2t"][ic * 128:(ic + 1) * 128, :]),
            waits=[(DV, self.g(f"ws_free_{slot}"))], inc=SW, amt=16)
        ld_w = SW.n
        ob = self.wa2st[:, slot * 1280:(slot + 1) * 1280]
        self.op("vector", lambda e, ws=ws, ob=ob: e.tensor_copy(out=ob, in_=ws),
                waits=[(SW, ld_w), (SWA, self.g(f"wa2_sp_{slot}"))], inc=DV)
        dv_w = DV.n
        self.t[f"ws_free_{slot}"] = dv_w
        self.op("gpsimd", lambda e, ic=ic, ob=ob: e.dma_start(
            out=self.wa2bf[ic * 128:(ic + 1) * 128, :], in_=ob),
            waits=[(DV, dv_w)], inc=SWA, amt=16)
        self.t[f"wa2_sp_{slot}"] = SWA.n

    def emit_a1(self):
        CC, PE, DV, AC = self.CC, self.PE, self.DV, self.AC
        pb = self.pb
        wready = self.wa1_ready

        def emit_lnx_load(tc):
            jblk = tc // 2
            half = tc % 2
            par = tc % 2
            cc_w = (self.cc_lnxt0 if tc == 0 else
                    (self.cc_lnxt1 if tc == 1 else 0))
            pe_free = self.g(f"a1_lnxfree_{par}")
            SL0 = self.S[f"lnx{par}"]
            if par == 0:
                for q in range(4):
                    self.op("sync", lambda e, jblk=jblk, half=half,
                            cc_w=cc_w, q=q: e.dma_start(
                        out=self.lnx.ap()[:, q * 4096:(q + 1) * 4096].rearrange(
                            "p (a t) -> p a t", t=TC),
                        in_=self.lnxt_all[half][
                            (jblk * NIC + q * 8) * P:
                            (jblk * NIC + q * 8 + 8) * P, :].rearrange(
                            "(a p) t -> p a t", p=128)),
                        waits=([(PE, pe_free), (CC, cc_w)] if q == 0 else []),
                        inc=SL0, amt=16)
            else:
                for hic, buf in ((0, self.lnxb0), (8, self.lnxb0),
                                 (16, self.lnxb1), (24, self.lnxb1)):
                    boff = (hic % 16) * TC
                    self.op("sync", lambda e, jblk=jblk, half=half, cc_w=cc_w,
                            hic=hic, buf=buf, boff=boff: e.dma_start(
                        out=buf[:, boff:boff + 4096].rearrange(
                            "p (a t) -> p a t", t=TC),
                        in_=self.lnxt_all[half][
                            (jblk * NIC + hic) * P:(jblk * NIC + hic + 8) * P,
                            :].rearrange("(a p) t -> p a t", p=128)),
                        waits=([(PE, pe_free), (CC, cc_w)] if hic == 0 else []),
                        inc=SL0, amt=16)
            self.t[f"a1_lnx_ld_{par}"] = SL0.n
        emit_lnx_load(0)
        wo_pieces = [(oc, hh) for hh in range(4) for oc in range(8)]
        self.cc_woutq = [0] * 4
        for tc in range(NTC):
            if tc + 1 < NTC:
                emit_lnx_load(tc + 1)
            ld_cur = self.t[f"a1_lnx_ld_{tc % 2}"]
            for pi in (2 * tc, 2 * tc + 1):
                oc, hh = wo_pieces[pi]
                extra = ([(AC, self.g(f"wsA_free_{pi % 2}"))] if pi < 2 else ())
                self.emit_wout_piece(oc, hh, pi % 2, extra)
                if oc == 7:
                    self.op("gpsimd", lambda e, hh=hh: e.collective_compute(
                        "AllGather", ALU.bypass, replica_groups=RG,
                        ins=[self.wout_in[hh].opt()],
                        outs=[self.wout_all[hh].opt()]),
                        waits=[(self.S["wo0"], self.S["wo0"].n),
                               (self.S["wo1"], self.S["wo1"].n)], inc=self.CC)
                    self.cc_woutq[hh] = self.CC.n
            self.emit_wa2_piece(2 * tc)
            self.emit_wa2_piece(2 * tc + 1)
            par = tc % 2
            SL0 = self.S[f"lnx{par}"]
            ld0 = ld_cur
            t0 = tc * TC

            def rhs_ap(ic, par=par):
                if par == 0:
                    return self.lnx.ap()[:, ic * TC:(ic + 1) * TC]
                if ic < 16:
                    return self.lnxb0[:, ic * TC:(ic + 1) * TC]
                return self.lnxb1[:, (ic - 16) * TC:(ic - 15) * TC]
            for ot in range(9):
                bank = pb[ot % 2]
                m = 128 if ot < 8 else NH
                for ic in range(NIC):
                    w0 = ic * WA1C + ot * 128
                    w = [(AC, wready[ic])] if tc == 0 else []
                    if ic == 0:
                        w += [(SL0, ld0),
                              (DV, self.g(f"a1_dvfree_{ot % 2}")),
                              (AC, self.g(f"a1_acfree_{ot % 2}"))]
                    rap = rhs_ap(ic)
                    self.op("tensor", lambda e, w0=w0, m=m, bank=bank,
                            ic=ic, rap=rap: e.matmul(
                        out=bank.ap()[0:m, :],
                        lhsT=self.wbuf.ap()[:, w0:w0 + m],
                        rhs=rap,
                        start=(ic == 0), stop=(ic == NIC - 1)),
                        waits=w, inc=(PE if ic == NIC - 1 else None))
                pe_ot = PE.n
                if ot == 8:
                    SD = self.S["ds0"]
                    self.op("vector", lambda e, bank=bank: e.tensor_copy(
                        out=self.dtev[:, 0:TC], in_=bank.ap()[0:NH, :]),
                        waits=[(PE, pe_ot), (SD, self.g("a1_dtsp"))], inc=DV)
                    self.t[f"a1_dvfree_{ot % 2}"] = DV.n
                    dv_dt = DV.n
                    self.op("gpsimd", lambda e, t0=t0: e.dma_start(
                        out=self.dt_spill[:, t0:t0 + TC], in_=self.dtev[:, 0:TC]),
                        waits=[(DV, dv_dt)], inc=SD, amt=16)
                    self.t["a1_dtsp"] = SD.n
                else:
                    zslot = ot % 2
                    SZ = self.S[f"zs{zslot}"]
                    zev = self.zevac if zslot == 0 else self.zevac2
                    self.op("scalar", lambda e, bank=bank, zev=zev: e.activation(
                        out=zev[:, 0:TC], in_=bank.ap()[:], func=AF.Copy),
                        waits=[(PE, pe_ot), (SZ, self.g(f"a1_zsp_{zslot}"))],
                        inc=AC)
                    self.t[f"a1_acfree_{ot % 2}"] = AC.n
                    ac_z = AC.n
                    self.op("gpsimd", lambda e, ot=ot, t0=t0, zev=zev: e.dma_start(
                        out=self.z_spill[ot * 128:(ot + 1) * 128, t0:t0 + TC],
                        in_=zev[:, 0:TC]),
                        waits=[(AC, ac_z)], inc=SZ, amt=16)
                    self.t[f"a1_zsp_{zslot}"] = SZ.n
            self.t[f"a1_lnxfree_{par}"] = PE.n
        self.a1_done_pe = PE.n
        self.a1_spill_fin = [(self.S["zs0"], self.S["zs0"].n),
                             (self.S["zs1"], self.S["zs1"].n),
                             (self.S["ds0"], self.S["ds0"].n),
                             (self.S["ds1"], self.S["ds1"].n)]
        self.wo_fin = [(self.S["wo0"], self.S["wo0"].n),
                       (self.S["wo1"], self.S["wo1"].n)]

    def emit_a2(self):
        CC, PE, DV, AC = self.CC, self.PE, self.DV, self.AC
        SDBG = self.S["dbg"]
        wconv_gate = [(PE, self.a1_done_pe)]
        if self.debug:
            self.op("gpsimd", lambda e: e.dma_start(
                out=self.dbgout["d_wbuf"], in_=self.wbuf.ap()[:, 0:NIC * WA1C]),
                waits=[(PE, self.a1_done_pe)], inc=SDBG, amt=16)
            wconv_gate.append((SDBG, SDBG.n))
        SWL = self.S["wa2l"]
        wa2_cache_done = [(self.S["wa2"], self.S["wa2"].n),
                          (self.S["cm"], self.S["cm"].n)]
        self.t["a2_ipfree_0"] = self.g("a1_acfree_0")
        self.t["a2_ipfree_1"] = self.g("a1_acfree_1")
        self.a1_dtev_dv = self.g("a1_dvfree_0")
        for ic in range(NIC):
            w = list(wconv_gate) + wa2_cache_done if ic < 2 else []
            self.op("sync", lambda e, ic=ic: e.dma_start(
                out=self.wbuf.ap()[:, ic * WA2C:(ic + 1) * WA2C],
                in_=self.wa2bf[ic * 128:(ic + 1) * 128, :]),
                waits=w, inc=SWL, amt=16)
        self.a2_wready_wait = (SWL, SWL.n)
        self.op("vector", lambda e: e.memset(self.S_in.ap()[:], 0.0),
                waits=[(PE, self.a1_done_pe)], inc=DV)
        self.op("vector", lambda e: e.memset(self.combo.ap()[:], 0.0), inc=DV)
        self.op("vector", lambda e: e.memset(self.S_bf.ap()[:], 0.0), inc=DV)
        self.op("vector", lambda e: e.memset(
            self.xbcpre.rearrange("p (g c) -> p g c", g=10)[:, :, 0:3], 0.0), inc=DV)
        self.dv_a2init = DV.n
        self.emit_ip_lnx_load(0)
        for part in range(4):
            self.emit_ip_part(0, part)
        self.emit_conv(0)
        for tc in range(NTC):
            dv_acs, sp_combo = self.emit_dt_chain(tc)
            if tc + 1 < NTC:
                self.emit_ip_lnx_load(tc + 1)
            ac_conv = self.g("a2_conv_ac")
            for ch in range(NCHTC):
                self.emit_chunk(tc, ch, dv_acs, sp_combo, ac_conv)
            if tc + 1 < NTC:
                for pp in range(4):
                    self.emit_ip_part(tc + 1, pp)
                self.emit_conv(tc + 1)
            self.emit_a2_tail(tc)
        self.a2_done_pe = self.PE.n
        self.a2_lnx_last = self.g("a2_lnxfree")
        self.op("gpsimd", lambda e: e.collective_compute(
            "AllReduce", ALU.add, replica_groups=RG,
            ins=[self.ssq_in.opt()], outs=[self.ssq_out.opt()]),
            waits=[(self.S["sq"], self.S["sq"].n)], inc=CC)
        self.cc_ssq = CC.n
        self.cc_y = CC.n
        if self.debug:
            d = self.dbgout
            pairs = [("d_z", self.z_spill),
                     ("d_dt", self.dt_spill), ("d_y", self.y_blk),
                     ("d_ssq", self.ssq_out)]
            for name, src in pairs:
                self.op("gpsimd", lambda e, name=name, src=src: e.dma_start(
                    out=d[name], in_=src), waits=[(CC, self.cc_y)],
                    inc=SDBG, amt=16)
            for h in range(2):
                for rb in range(4):
                    self.op("gpsimd", lambda e, h=h, rb=rb: e.dma_start(
                        out=d["d_lnxt"][rb * 8192:(rb + 1) * 8192,
                                        h * TC:(h + 1) * TC],
                        in_=self.lnxt_all[h][rb * 8192:(rb + 1) * 8192, :]),
                        waits=[(CC, self.cc_y)], inc=SDBG, amt=16)

    def emit_ip_lnx_load(self, tc):
        jblk = tc // 2
        half = tc % 2
        pe_free = self.g("a2_lnxfree", self.a1_done_pe)
        SL = self.S["lnx0"]
        for q in range(4):
            self.op("sync", lambda e, jblk=jblk, half=half, q=q: e.dma_start(
                out=self.lnx.ap()[:, q * 4096:(q + 1) * 4096].rearrange(
                    "p (a t) -> p a t", t=TC),
                in_=self.lnxt_all[half][
                    (jblk * NIC + q * 8) * P:
                    (jblk * NIC + q * 8 + 8) * P, :].rearrange(
                    "(a p) t -> p a t", p=128)),
                waits=([(self.PE, pe_free)] if q == 0 else []),
                inc=SL, amt=16)
        self.t["a2_lnx_ld"] = SL.n

    def emit_ip_part(self, tc, part):
        PE, DV, AC = self.PE, self.DV, self.AC
        pb = self.pb
        ots = [(0, 3), (3, 6), (6, 8), (8, 10)][part]
        if part == 0:
            if tc > 0:
                self.op("vector", lambda e: e.tensor_copy(
                    out=self.xbcpre.rearrange("p (g c) -> p g c", g=10)[:, :, 0:3],
                    in_=self.xbcpre.rearrange("p (g c) -> p g c", g=10)[:, :, 512:515]),
                    waits=[(PE, self.g("a2_conv_pe"))], inc=DV)
                self.t["a2_hist_dv"] = DV.n
            else:
                self.t["a2_hist_dv"] = self.dv_a2init
        dv_hist = self.g("a2_hist_dv")
        SL0 = self.S["lnx0"]
        ld0 = self.g("a2_lnx_ld")
        for ot in range(*ots):
            bank = pb[ot % 2]
            for ic in range(NIC):
                w0 = ic * WA2C + ot * 128
                w = ([self.a2_wready_wait]
                     if (tc == 0 and part == 0 and ic == 0 and ot == 0) else [])
                if ic == 0:
                    w += [(SL0, ld0), (DV, self.dv_a2init),
                          (AC, self.g(f"a2_ipfree_{ot % 2}")),
                          (DV, max(dv_hist, self.a1_dtev_dv))]
                self.op("tensor", lambda e, w0=w0, bank=bank, ic=ic: e.matmul(
                    out=bank.ap()[:],
                    lhsT=self.wbuf.ap()[:, w0:w0 + 128],
                    rhs=self.lnx.ap()[:, ic * TC:(ic + 1) * TC],
                    start=(ic == 0), stop=(ic == NIC - 1)),
                    waits=w, inc=(PE if ic == NIC - 1 else None))
            pe_ot = PE.n
            self.op("scalar", lambda e, ot=ot, bank=bank: e.activation(
                out=self.xbcpre[:, ot * 515 + 3:ot * 515 + 3 + TC],
                in_=bank.ap()[:], func=AF.Copy),
                waits=[(PE, pe_ot)], inc=AC)
            self.t[f"a2_ipfree_{ot % 2}"] = AC.n
        if part == 3:
            self.t["a2_lnxfree"] = PE.n
            self.t["a2_ip_ac"] = AC.n

    def emit_conv(self, tc):
        PE, DV, AC = self.PE, self.DV, self.AC
        pb = self.pb
        ac_ip = self.g("a2_ip_ac")
        for g in range(10):
            bank = pb[2 + (g % 2)]
            for k in range(4):
                w = []
                if k == 0:
                    w = [(AC, ac_ip), (AC, self.g(f"a2_cvfree_{g % 2}")),
                         (DV, self.dv_diag),
                         (DV, self.g("a2_st_dv")),
                         (DV, self.g("a2_yo_free")),
                         (DV, self.g("a2_ctok_free"))]
                self.op("tensor", lambda e, g=g, k=k, bank=bank: e.matmul(
                    out=bank.ap()[:],
                    lhsT=self.dgw.ap()[:, (g * 4 + k) * P:(g * 4 + k + 1) * P],
                    rhs=self.xbcpre[:, g * 515 + k:g * 515 + k + TC],
                    start=(k == 0), stop=(k == 3)),
                    waits=(w if k == 0 else []),
                    inc=(PE if k == 3 else None))
            pe_cv = PE.n
            self.op("scalar", lambda e, g=g, bank=bank: e.activation(
                out=self.xbcpost[:, g * 512:(g + 1) * 512],
                in_=bank.ap()[:], func=AF.Silu, bias=self.cb[:, g:g + 1]),
                waits=[(PE, pe_cv)], inc=AC)
            self.t[f"a2_cvfree_{g % 2}"] = AC.n
        self.t["a2_conv_pe"] = PE.n
        self.t["a2_conv_ac"] = AC.n

    def emit_dt_chain(self, tc):
        PE, DV, AC = self.PE, self.DV, self.AC
        SDTR = self.S["dtr"]
        self.op("sync", lambda e, tc=tc: e.dma_start(
            out=self.dtr, in_=self.dt_spill[:, tc * TC:(tc + 1) * TC]),
            waits=self.a1_spill_fin + [(DV, self.g("a2_dt_free")),
                   (AC, self.g("a2_dtac")), (PE, self.g("a2_ssqmm_pe")),
                   (DV, self.g("a2_st_dv"))],
            inc=SDTR, amt=16)
        ld_dt = SDTR.n
        self.op("scalar", lambda e: e.activation(
            out=self.sp1, in_=self.dtr, func=AF.Abs,
            bias=self.dtbias[0:NH, :]),
            waits=[(SDTR, ld_dt), (DV, self.g("a2_yev_dv")),
                   (AC, AC.n)], inc=AC)
        self.t["a2_dtac"] = AC.n
        self.op("scalar", lambda e: e.activation(
            out=self.sp1, in_=self.sp1, func=AF.Exp, scale=-1.0),
            waits=[(AC, AC.n)], inc=AC)
        self.op("scalar", lambda e: e.activation(
            out=self.sp1, in_=self.sp1, func=AF.Ln, bias=1.0),
            waits=[(AC, AC.n)], inc=AC)
        ac_sp = AC.n
        self.t["a2_sp_ac"] = ac_sp
        self.op("vector", lambda e: e.tensor_scalar(
            out=self.sp2[:], in0=self.dtr, scalar1=self.dtbias[0:NH, :],
            scalar2=0.0, op0=ALU.add, op1=ALU.max), waits=[(SDTR, ld_dt)], inc=DV)
        self.op("vector", lambda e: e.tensor_tensor(
            out=self.combo.ap()[32:48, :], in0=self.sp1, in1=self.sp2[:],
            op=ALU.add),
            waits=[(AC, ac_sp), (PE, self.g("a2_ct_pe")), (DV, DV.n)], inc=DV)
        self.t["a2_dt_free"] = DV.n
        self.op("vector", lambda e: e.tensor_scalar(
            out=self.cumA, in0=self.combo.ap()[32:48, :],
            scalar1=self.apos[32:32 + NH, :], scalar2=None, op0=ALU.mult),
            waits=[(AC, self.ac_apos), (DV, DV.n)], inc=DV)
        self.op("vector", lambda e: e.tensor_tensor_scan(
            out=self.combo.ap()[0:NH, :], data0=self.scanmask.ap()[:],
            data1=self.cumA, initial=0.0, op0=ALU.mult, op1=ALU.add),
            waits=[(PE, self.g("a2_ct_pe")), (DV, DV.n)], inc=DV)
        dv_acs = DV.n
        SCM = self.S["chl"]
        self.op("vector", lambda e: e.tensor_copy(
            out=self.chi, in_=self.combo.ap()[0:NH, :]),
            waits=[(DV, dv_acs), (PE, self.g("a2_ytr_pe")),
                   (SCM, self.g("a2_chl_sp"))], inc=DV)
        self.op("vector", lambda e: e.tensor_tensor(
            out=self.clo, in0=self.combo.ap()[0:NH, :], in1=self.chi,
            op=ALU.subtract), waits=[(DV, DV.n)], inc=DV)
        dv_chl = DV.n
        self.op("gpsimd", lambda e: e.dma_start(
            out=self.combo_drb[0:NH, :], in_=self.chi),
            waits=[(DV, dv_chl), (self.S["fl"], self.g("a2_flat_sp"))],
            inc=SCM, amt=16)
        self.op("gpsimd", lambda e: e.dma_start(
            out=self.combo_drb[NH:2 * NH, :], in_=self.clo), inc=SCM, amt=16)
        self.t["a2_chl_sp"] = SCM.n
        for ch in range(NCHTC):
            w = []
            if ch == 0:
                w = [(DV, dv_acs), (DV, self.g("a2_ctok_free")),
                     (DV, self.g("a2_st_dv")), (AC, self.g("a2_conv_ac"))]
            self.op("tensor", lambda e, ch=ch: e.transpose(
                out=self.pb[3].ap()[:, ch * CH:(ch + 1) * CH],
                in_=self.combo.ap()[:, ch * CH:(ch + 1) * CH],
                identity=self.identf[:]),
                waits=w, inc=(PE if ch == NCHTC - 1 else None))
        pe_ct = PE.n
        self.t["a2_ct_pe"] = pe_ct
        self.op("vector", lambda e: e.tensor_copy(
            out=self.ctok.ap()[:], in_=self.pb[3].ap()[:]),
            waits=[(PE, pe_ct), (AC, self.g("a2_ctok_ac_war"))], inc=DV)
        self.t["a2_ctok_dvct"] = DV.n
        self.t["a2_ctok_free"] = DV.n
        return dv_acs, SCM.n

    def emit_a2_tail(self, tc):
        CC, DV = self.CC, self.DV
        self.op("gpsimd", lambda e, tc=tc: e.dma_start(
            out=self.ssq_in[:, tc * TC:(tc + 1) * TC], in_=self.ssqr.ap()[0:1, :]),
            waits=[(DV, self.g("a2_ssq_dv"))], inc=self.S["sq"], amt=16)
        self.t["a2_ssqr_sp"] = self.S["sq"].n
        self.op("gpsimd", lambda e, tc=tc: e.dma_start(
            out=self.y_blk[tc * DIC:(tc + 1) * DIC, :].rearrange(
                "(a p) t -> p a t", p=128),
            in_=self.yfm.rearrange("p (a t) -> p a t", a=8)),
            waits=[(DV, self.g("a2_gate_dv"))], inc=self.S["yi"], amt=16)
        self.t["a2_y_sp"] = self.S["yi"].n
        self.op("gpsimd", lambda e, tc=tc: e.collective_compute(
            "AllGather", ALU.bypass, replica_groups=RG,
            ins=[self.y_blk[tc * DIC:(tc + 1) * DIC, :].opt()],
            outs=[self.y_allb[tc * NCORES * DIC:(tc + 1) * NCORES * DIC, :].opt()]),
            waits=[(self.S["yi"], self.S["yi"].n)], inc=CC)

    def emit_chunk(self, tc, ch, dv_acs, sp_combo, ac_conv):
        CC, PE, DV, AC = self.CC, self.PE, self.DV, self.AC
        SFL, SCM = self.S["fl"], self.S["chl"]
        pb = self.pb
        xoff = ch * CH
        B_fm = self.xbcpost[:, 8 * 512 + xoff:8 * 512 + xoff + CH]
        C_fm = self.xbcpost[:, 9 * 512 + xoff:9 * 512 + xoff + CH]
        dv_ct = self.g("a2_ctok_dvct")
        negacs_cols = self.ctok.ap()[:, ch * CH:ch * CH + NH]
        dt_cols = self.ctok.ap()[:, ch * CH + 32:ch * CH + 32 + NH]
        self.op("gpsimd", lambda e, ch=ch: e.dma_start(
            out=self.fbf.rearrange("o (g t) -> o g t", g=2 * NH),
            in_=self.combo_drb[:, ch * CH:(ch + 1) * CH].unsqueeze(0)),
            waits=[(SCM, sp_combo), (PE, self.g("a2_flat_pe"))], inc=SFL, amt=16)
        sp_flat = SFL.n
        self.t["a2_flat_sp"] = sp_flat
        for t in range(8):
            self.op("tensor", lambda e, t=t, xoff=xoff: e.transpose(
                out=self.pb6.ap()[:, t * 128:(t + 1) * 128],
                in_=self.xbcpost[:, t * 512 + xoff:t * 512 + xoff + CH],
                identity=self.identb[:]),
                waits=([(AC, ac_conv), (DV, self.g("a2_xtr_free")),
                        (DV, self.g("a2_gate_dv"))] if t == 0 else []),
                inc=(PE if t == 7 else None))
        pe_xtr = PE.n
        self.t["a2_xtr_pe"] = pe_xtr
        dtb_ap = dt_cols.unsqueeze(2).broadcast_to([P, NH, HD])
        self.op("vector", lambda e, dtb_ap=dtb_ap: e.tensor_tensor(
            out=self.xdt.rearrange("p (h d) -> p h d", h=NH),
            in0=self.pb6.ap().rearrange("p (h d) -> p h d", h=NH),
            in1=dtb_ap, op=ALU.mult),
            waits=[(PE, pe_xtr), (DV, dv_ct), (PE, self.g("a2_yd_pe")),
                   (DV, self.g("a2_xdtd_dv"))], inc=DV)
        dv_xdt = DV.n
        dbc_ap = self.dbc.ap().unsqueeze(2).broadcast_to([P, NH, HD])
        self.op("vector", lambda e, dbc_ap=dbc_ap: e.tensor_tensor(
            out=self.xD.rearrange("p (h d) -> p h d", h=NH),
            in0=self.pb6.ap().rearrange("p (h d) -> p h d", h=NH),
            in1=dbc_ap, op=ALU.mult),
            waits=[(DV, self.dv_dbc), (DV, self.g("a2_yev_dv"))], inc=DV)
        self.t["a2_xtr_free"] = DV.n
        dv_xD = DV.n
        negT_hi = self.fbf[0:1, 0:2048].rearrange(
            "o (h t) -> o h t", t=CH)[:, :, 127:128].squeeze(2)
        negT_lo = self.fbf[0:1, 2048:4096].rearrange(
            "o (h t) -> o h t", t=CH)[:, :, 127:128].squeeze(2)
        self.op("tensor", lambda e, negT_hi=negT_hi: e.matmul(
            out=pb[4].ap()[:, 0:NH], lhsT=self.ones_bf_row(),
            rhs=negT_hi, start=True, stop=False),
            waits=[(SFL, sp_flat), (AC, self.g("a2_ab_free_0")),
                   (DV, self.g("a2_ssqev_dv"))], inc=None)
        self.op("tensor", lambda e, negT_lo=negT_lo: e.matmul(
            out=pb[4].ap()[:, 0:NH], lhsT=self.ones_bf_row(),
            rhs=negT_lo, start=False, stop=True), waits=[], inc=PE)
        pe_negT = PE.n
        self.op("vector", lambda e: e.tensor_tensor(
            out=self.dsdiff[:], in0=negacs_cols, in1=pb[4].ap()[:, 0:NH],
            op=ALU.subtract),
            waits=[(PE, pe_negT), (AC, self.g("a2_ds_ac")), (DV, dv_ct)], inc=DV)
        dv_dsd = DV.n
        self.op("scalar", lambda e: e.activation(
            out=self.expTb[:], in_=pb[4].ap()[:, 0:NH], func=AF.Exp, scale=-1.0),
            waits=[(PE, pe_negT), (DV, self.g("a2_scan_a_dv"))], inc=AC)
        ac_expT = AC.n
        self.op("scalar", lambda e: e.activation(
            out=self.dstok[:], in_=self.dsdiff[:], func=AF.Exp),
            waits=[(DV, dv_dsd), (DV, self.g("a2_xdtd_dv"))], inc=AC)
        ac_ds = AC.n
        self.t["a2_ds_ac"] = ac_ds
        self.op("scalar", lambda e, negacs_cols=negacs_cols: e.activation(
            out=self.atok[:], in_=negacs_cols, func=AF.Exp, scale=-1.0),
            waits=[(DV, dv_ct), (DV, self.g("a2_yev_dv"))], inc=AC)
        ac_atok = AC.n
        ds_ap = self.dstok.ap().unsqueeze(2).broadcast_to([P, NH, HD])
        self.op("vector", lambda e, ds_ap=ds_ap: e.tensor_tensor(
            out=self.xdtd.rearrange("p (h d) -> p h d", h=NH),
            in0=self.xdt.rearrange("p (h d) -> p h d", h=NH),
            in1=ds_ap, op=ALU.mult),
            waits=[(AC, ac_ds), (DV, dv_xdt), (PE, self.g("a2_st_pe"))], inc=DV)
        dv_xdtd = DV.n
        self.t["a2_xdtd_dv"] = dv_xdtd
        self.op("tensor", lambda e, B_fm=B_fm, C_fm=C_fm: e.matmul(
            out=pb[2].ap()[:, 0:128], lhsT=B_fm, rhs=C_fm, start=True, stop=True),
            waits=[(AC, ac_conv), (DV, self.g("a2_gm_dv")),
                   (DV, self.g("a2_st_dv"))], inc=PE)
        pe_g = PE.n
        self.t["a2_g_pe"] = pe_g
        self.op("vector", lambda e: e.tensor_tensor(
            out=self.Gm.ap()[:], in0=pb[2].ap()[:, 0:128], in1=self.trimask[:],
            op=ALU.mult), waits=[(PE, pe_g)], inc=DV)
        dv_gm = DV.n
        self.t["a2_gm_dv"] = dv_gm
        self.op("tensor", lambda e, B_fm=B_fm: e.transpose(
            out=self.pb6.ap()[:, 0:128], in_=B_fm, identity=self.identb[:]),
            waits=[(DV, self.g("a2_xtr_free")), (DV, self.g("a2_btok_dv"))],
            inc=PE)
        pe_bt = PE.n
        self.op("vector", lambda e: e.tensor_copy(
            out=self.Btok.ap()[:], in_=self.pb6.ap()[:, 0:128]),
            waits=[(PE, pe_bt)], inc=DV)
        dv_btok = DV.n
        self.t["a2_btok_dv"] = dv_btok
        for r in range(2):
            self.op("tensor", lambda e, C_fm=C_fm, r=r: e.matmul(
                out=pb[3].ap()[:], lhsT=C_fm,
                rhs=self.S_bf.ap()[:, r * 512:(r + 1) * 512],
                start=True, stop=True),
                waits=[(AC, ac_conv), (DV, self.g("a2_sbf_dv")),
                       (DV, self.g("a2_yo_free")), (DV, dv_ct)], inc=PE)
            pe_yo = PE.n
            self.t["a2_yo_pe"] = pe_yo
            for q in (2 * r, 2 * r + 1):
                abank = pb[4] if q % 2 == 0 else pb[5]
                w = [(SFL, sp_flat), (AC, self.g(f"a2_ab_free_{q % 2}"))]
                if q == 0:
                    w += [(DV, dv_dsd), (AC, ac_expT),
                          (DV, self.g("a2_ssqev_dv"))]
                self.op("tensor", lambda e, q=q, abank=abank: e.matmul(
                    out=abank.ap()[:], lhsT=self.ones_bf_row(),
                    rhs=self.fbf[0:1, q * 512:(q + 1) * 512],
                    start=True, stop=False), waits=w, inc=None)
                self.op("tensor", lambda e, q=q, abank=abank: e.matmul(
                    out=abank.ap()[:], lhsT=self.ones_bf_row(),
                    rhs=self.fbf[0:1, 2048 + q * 512:2048 + (q + 1) * 512],
                    start=False, stop=True), waits=[], inc=PE)
                pe_ab = PE.n
                if q == 3:
                    self.t["a2_flat_pe"] = PE.n
                for es in range(4):
                    h = q * 4 + es
                    self.op("scalar", lambda e, h=h, es=es, abank=abank,
                            negacs_cols=negacs_cols: e.activation(
                        out=self.Ebuf.ap()[:, es * P:(es + 1) * P],
                        in_=abank.ap()[:, es * 128:(es + 1) * 128],
                        func=AF.Exp, scale=-1.0, bias=negacs_cols[:, h:h + 1]),
                        waits=([(PE, pe_ab), (DV, self.g("a2_m_dv")),
                                (DV, dv_ct)] if es == 0
                               else [(AC, AC.n)]), inc=AC)
                self.t[f"a2_ab_free_{q % 2}"] = AC.n
                ac_e = AC.n
                self.op("vector", lambda e: e.scalar_tensor_tensor(
                    out=self.Mbuf.ap().rearrange("p (h l) -> p h l", h=4),
                    in0=self.Ebuf.ap().rearrange("p (h l) -> p h l", h=4),
                    scalar=1.0,
                    in1=self.Gm.ap().unsqueeze(1).broadcast_to([P, 4, P]),
                    op0=ALU.min, op1=ALU.mult),
                    waits=[(AC, ac_e), (DV, dv_gm),
                           (PE, self.g("a2_yd_pe"))], inc=DV)
                self.t["a2_m_dv"] = DV.n
                dv_m = DV.n
                for es in range(4):
                    h = q * 4 + es
                    hh = h - r * 8
                    w = [(DV, dv_m), (DV, dv_xdt)]
                    if hh == 0:
                        w.append((DV, self.g("a2_yd_free")))
                    self.op("tensor", lambda e, hh=hh, h=h, es=es: e.matmul(
                        out=self.pb7.ap()[:, hh * HD:(hh + 1) * HD],
                        lhsT=self.Mbuf.ap()[:, es * P:(es + 1) * P],
                        rhs=self.xdt[:, h * HD:(h + 1) * HD],
                        start=True, stop=True),
                        waits=(w if es == 0 else []),
                        inc=(PE if es == 3 else None))
                self.t["a2_yd_pe"] = PE.n
            pe_yd = PE.n
            a_ap = self.atok.ap()[:, r * 8:(r + 1) * 8].unsqueeze(2).broadcast_to(
                [P, 8, HD])
            self.op("vector", lambda e, a_ap=a_ap: e.tensor_tensor(
                out=self.yscr.ap().rearrange("p (h d) -> p h d", h=8),
                in0=pb[3].ap().rearrange("p (h d) -> p h d", h=8),
                in1=a_ap, op=ALU.mult),
                waits=[(PE, pe_yo), (AC, ac_atok),
                       (AC, self.g("a2_sp_ac"))], inc=DV)
            self.t["a2_yo_free"] = DV.n
            self.op("vector", lambda e: e.tensor_tensor(
                out=self.yscr.ap()[:], in0=self.pb7.ap()[:], in1=self.yscr.ap()[:],
                op=ALU.add), waits=[(PE, pe_yd), (DV, DV.n)], inc=DV)
            self.t["a2_yd_free"] = DV.n
            ytw = [(DV, dv_xD), (PE, self.g("a2_ytr_pe")), (DV, DV.n)]
            if ch == 0:
                ytw.append((SCM, self.g("a2_chl_sp")))
            self.op("vector", lambda e, r=r: e.tensor_tensor(
                out=self.ytok[:, r * 512:(r + 1) * 512],
                in0=self.yscr.ap()[:], in1=self.xD[:, r * 512:(r + 1) * 512],
                op=ALU.add),
                waits=ytw,
                inc=DV)
            self.t["a2_yev_dv"] = DV.n
        dv_yev = DV.n
        for r in range(2):
            stbank = pb[2] if r == 0 else pb[3]
            stw = [(DV, dv_btok), (DV, dv_xdtd)]
            if r == 0:
                stw.append((DV, dv_gm))
            else:
                stw.append((DV, self.g("a2_yo_free")))
            self.op("tensor", lambda e, r=r, stbank=stbank: e.matmul(
                out=stbank.ap()[:], lhsT=self.Btok.ap()[:],
                rhs=self.xdtd[:, r * 512:(r + 1) * 512],
                start=True, stop=True), waits=stw, inc=PE)
            self.t[f"a2_st_pe_{r}"] = PE.n
            self.t["a2_st_pe"] = PE.n
        expT_ap = self.expTb.ap().unsqueeze(2).broadcast_to([P, NH, HD])
        self.op("vector", lambda e, expT_ap=expT_ap: e.tensor_tensor(
            out=self.S_a.ap().rearrange("p (h d) -> p h d", h=NH),
            in0=self.S_in.ap().rearrange("p (h d) -> p h d", h=NH),
            in1=expT_ap, op=ALU.mult),
            waits=[(AC, ac_expT), (PE, self.g("a2_ssqmm_pe"))], inc=DV)
        self.t["a2_scan_a_dv"] = DV.n
        dv_sa = DV.n
        for r in range(2):
            stbank = pb[2] if r == 0 else pb[3]
            self.op("vector", lambda e, r=r, stbank=stbank: e.tensor_tensor(
                out=self.S_in.ap()[:, r * 512:(r + 1) * 512],
                in0=self.S_a.ap()[:, r * 512:(r + 1) * 512],
                in1=stbank.ap()[:], op=ALU.add),
                waits=[(PE, self.g(f"a2_st_pe_{r}")), (DV, dv_sa)], inc=DV)
        self.t["a2_st_dv"] = DV.n
        dv_sti = DV.n
        self.op("vector", lambda e: e.tensor_copy(
            out=self.S_bf.ap()[:], in_=self.S_in.ap()[:]),
            waits=[(PE, self.g("a2_yo_pe")), (DV, dv_sti)], inc=DV)
        self.t["a2_sbf_dv"] = DV.n
        for t in range(8):
            self.op("tensor", lambda e, t=t: e.transpose(
                out=self.pb6.ap()[:, t * 128:(t + 1) * 128],
                in_=self.ytok[:, t * 128:(t + 1) * 128],
                identity=self.identb[:]),
                waits=([(DV, dv_yev), (DV, dv_btok),
                        (DV, self.g("a2_gate_dv"))] if t == 0 else []),
                inc=(PE if t == 7 else None))
        pe_ytr = PE.n
        self.t["a2_ytr_pe"] = pe_ytr
        for grp in range(2):
            SZT = self.S[f"z{grp}"]
            zlw = [(AC, self.g("a2_zs_ac"))]
            if tc == 0 and ch == 0:
                zlw += self.a1_spill_fin
            self.op("sync", lambda e, grp=grp, tc=tc, ch=ch: e.dma_start(
                out=self.zstg.rearrange("p (a t) -> p a t", t=CH),
                in_=self.z_spill[grp * 512:(grp + 1) * 512,
                                 tc * TC + ch * CH:tc * TC + (ch + 1) * CH].rearrange(
                    "(a p) t -> p a t", p=128)),
                waits=zlw, inc=SZT, amt=16)
            ld_zt = SZT.n
            zw = [(SZT, ld_zt), (DV, self.g("a2_gate_dv"))]
            if tc == 0 and ch == 0 and grp == 0:
                zw += self.wo_fin
            self.op("scalar", lambda e, grp=grp, ch=ch: e.activation(
                out=self.siluz.rearrange("p (a t) -> p a t", t=512)[
                    :, grp * 4:(grp + 1) * 4, ch * CH:(ch + 1) * CH],
                in_=self.zstg.rearrange("p (a t) -> p a t", t=CH),
                func=AF.Silu), waits=zw, inc=AC)
            self.t["a2_zs_ac"] = AC.n
        ac_z = AC.n
        gw = [(PE, pe_ytr), (AC, ac_z), (self.S["yi"], self.g("a2_y_sp"))]
        if tc == 0 and ch == 0:
            gw += self.wo_fin
        self.op("vector", lambda e, ch=ch: e.tensor_tensor(
            out=self.yfm.rearrange("p (a t) -> p a t", a=8)[:, :, ch * CH:(ch + 1) * CH],
            in0=self.pb6.ap().rearrange("p (a t) -> p a t", a=8),
            in1=self.siluz.rearrange("p (a t) -> p a t", a=8)[:, :, ch * CH:(ch + 1) * CH],
            op=ALU.mult), waits=gw, inc=DV)
        self.t["a2_gate_dv"] = DV.n
        dv_gate = DV.n
        self.op("scalar", lambda e, ch=ch: e.activation(
            out=self.sqs_bf.rearrange("p (a t) -> p a t", a=8),
            in_=self.yfm.rearrange("p (a t) -> p a t", a=8)[:, :, ch * CH:(ch + 1) * CH],
            func=AF.Square),
            waits=[(DV, dv_gate), (PE, self.g("a2_ssqmm_pe")),
                   (DV, self.g("a2_st_dv"))], inc=AC)
        ac_sq = AC.n
        for t in range(8):
            w = []
            if t == 0:
                w = [(AC, ac_sq), (AC, self.g("a2_ab_free_0")),
                     (DV, self.g("a2_ssqev_dv"))]
            self.op("tensor", lambda e, t=t: e.matmul(
                out=pb[4].ap()[0:1, 0:128],
                lhsT=self.trimask.ap()[:, 127:128],
                rhs=self.sqs_bf[:, t * 128:(t + 1) * 128],
                start=(t == 0), stop=(t == 7)), waits=w,
                inc=(PE if t == 7 else None))
        pe_ssq = PE.n
        self.t["a2_ssqmm_pe"] = pe_ssq
        self.op("vector", lambda e, ch=ch: e.tensor_copy(
            out=self.ssqr.ap()[0:1, ch * CH:(ch + 1) * CH],
            in_=pb[4].ap()[0:1, 0:128]),
            waits=[(PE, pe_ssq), (self.S["sq"], self.g("a2_ssqr_sp"))], inc=DV)
        self.t["a2_ssqev_dv"] = DV.n
        self.t["a2_ssq_dv"] = DV.n
        self.t["a2_ctok_ac_war"] = self.AC.n
        if self.debug and tc == 0:
            SDBG = self.S["dbg"]
            d = self.dbgout
            for name, srcap in [("d_xdt", self.xdt), ("d_xdtd", self.xdtd),
                                ("d_ytok", self.ytok)]:
                self.op("gpsimd", lambda e, name=name, srcap=srcap, ch=ch: e.dma_start(
                    out=d[name][:, ch * 1024:(ch + 1) * 1024], in_=srcap),
                    waits=[(DV, self.DV.n)], inc=SDBG, amt=16)
            self.op("gpsimd", lambda e, ch=ch: e.dma_start(
                out=d["d_sbf"][:, ch * 1024:(ch + 1) * 1024], in_=self.S_bf[:]),
                waits=[(DV, self.DV.n)], inc=SDBG, amt=16)
            if ch == NCHTC - 1:
                self.op("gpsimd", lambda e: e.dma_start(
                    out=d["d_combo"], in_=self.combo[:]),
                    waits=[(DV, self.DV.n)], inc=SDBG, amt=16)
                self.op("gpsimd", lambda e: e.dma_start(
                    out=d["d_ctok"], in_=self.ctok[:]),
                    waits=[(DV, self.DV.n)], inc=SDBG, amt=16)
            self.t["a2_yev_dv"] = self.t["a2_yev_dv"]

    def emit_b(self):
        di = self.di
        CC, PE, DV, AC = self.CC, self.PE, self.DV, self.AC
        SBY, SOS, SDBG = self.S["by"], self.S["os"], self.S["dbg"]
        pb = self.pb
        tokv = {}
        def load_tok(e):
            with e.register("tok0r") as r, e.register("tok1r") as r2, \
                 e.register("tok2r") as r3:
                e.reg_load(r, di["tok0"][0:1, 0:1])
                tokv["v"] = e.snap(r, min_val=0, max_val=L - 1024)
                e.reg_mul(r2, tokv["v"], 16)
                tokv["q0"] = e.snap(r2, min_val=0, max_val=16 * (L - 1024))
                for qq in range(1, 4):
                    e.reg_add(r3, tokv["q0"], qq * 4096)
                    tokv[f"q{qq}"] = e.snap(
                        r3, min_val=qq * 4096,
                        max_val=16 * (L - 1024) + qq * 4096)
            return e.nop()
        self.ops["gpsimd"].append(lambda e: load_tok(e))
        self.b_tok = tokv

        def drive(g, n):
            for _ in range(n):
                try:
                    next(g)
                except StopIteration:
                    return False
            return True
        pregen = self.emit_c_pre_gen()

        def cj_all(js):
            for jj in js:
                yield from self.emit_c_j_gen(jj)
        cj03 = cj_all(range(4))
        for th in range(2):
            w = [(CC, self.cc_y), (PE, self.g("b_yfree"))]
            if th == 0:
                w += [(PE, self.a2_done_pe)]
            for qq in range(2):
                def ydma(e, th=th, qq=qq):
                    off = tokv[f"q{th * 2 + qq}"]
                    return e.dma_start(
                        out=self.b_y[:, qq * 16384:(qq + 1) * 16384].rearrange(
                            "p (a t) -> p a t", t=512),
                        in_=self.y_allb[bass.ds(off, 4096), :].rearrange(
                            "(a p) t -> p a t", p=128))
                self.op("gpsimd", ydma, waits=(w if qq == 0 else []),
                        inc=SBY, amt=16)
            sp_y = SBY.n
            for oc in range(32):
                slot = oc % 2
                wdst = self.b_w[:, slot * 8192:(slot + 1) * 8192]
                SBW = self.S[f"bw{slot}"]
                for qq in range(2):
                    self.op("sync", lambda e, oc=oc, wdst=wdst, qq=qq: e.dma_start(
                        out=wdst[:, qq * 4096:(qq + 1) * 4096].rearrange(
                            "p (a t) -> p a t", t=128),
                        in_=self.wout_all[oc // 8][
                            qq * 4096:(qq + 1) * 4096,
                            (oc % 8) * 128:(oc % 8 + 1) * 128].rearrange(
                            "(a p) t -> p a t", p=128)),
                        waits=([(CC, self.cc_woutq[oc // 8]),
                                (PE, self.g(f"b_wfree_{slot}")),
                                (PE, self.a2_lnx_last
                                 if (th == 0 and oc < 2) else 0)]
                               if qq == 0 else []),
                        inc=SBW, amt=16)
                ld_w = SBW.n
                mmbank = pb[oc % 2]
                for fc in range(64):
                    w = []
                    if fc == 0:
                        w = [(SBW, ld_w), (SBY, sp_y),
                             (DV, self.g(f"b_mmfree_{oc % 2}"))]
                    self.op("tensor", lambda e, fc=fc, wdst=wdst, mmbank=mmbank: e.matmul(
                        out=mmbank.ap()[:],
                        lhsT=wdst[:, fc * 128:(fc + 1) * 128],
                        rhs=self.b_y[:, fc * 512:(fc + 1) * 512],
                        start=(fc == 0), stop=(fc == 63)),
                        waits=w, inc=(PE if fc == 63 else None))
                pe_mm = PE.n
                self.t[f"b_wfree_{slot}"] = pe_mm
                bsb = self.b_sb2[oc % 2]
                self.op("vector", lambda e, mmbank=mmbank, bsb=bsb: e.tensor_copy(
                    out=bsb, in_=mmbank.ap()[:]),
                    waits=[(PE, pe_mm), (PE, self.g("b_trpe")),
                           (self.S["yi"], self.g("a2_y_sp") if th == 0 and oc == 0 else 0),
                           (DV, self.g(f"b_sbfree_{oc % 2}"))], inc=DV)
                dv_sb = DV.n
                self.t[f"b_mmfree_{oc % 2}"] = dv_sb
                self.t[f"b_sbfree_{oc % 2}"] = dv_sb
                trbank = pb[2 + (oc % 2)]
                for q in range(4):
                    self.op("tensor", lambda e, q=q, trbank=trbank,
                            bsb=bsb: e.transpose(
                        out=trbank.ap()[:, q * 128:(q + 1) * 128],
                        in_=bsb[:, q * 128:(q + 1) * 128],
                        identity=self.identf[:]),
                        waits=([(DV, dv_sb), (DV, self.g(f"b_ttsp_{oc % 2}"))]
                               if q == 0 else []),
                        inc=(PE if q == 3 else None))
                pe_tr = PE.n
                self.t["b_trpe"] = pe_tr
                btt = self.b_tt2[oc % 2]
                self.op("vector", lambda e, trbank=trbank, btt=btt: e.tensor_copy(
                    out=btt, in_=trbank.ap()[:]),
                    waits=[(PE, pe_tr), (SOS, self.g(f"b_ttdma_{oc % 2}"))],
                    inc=DV)
                dv_tt = DV.n
                self.t[f"b_ttsp_{oc % 2}"] = dv_tt
                self.op("gpsimd", lambda e, th=th, oc=oc, btt=btt: e.dma_start(
                    out=self.out_spill[th * 512:(th + 1) * 512,
                                       oc * 128:(oc + 1) * 128].rearrange(
                        "(q p) o -> p q o", p=128),
                    in_=btt.rearrange("p (q o) -> p q o", q=4)),
                    waits=[(DV, dv_tt)], inc=SOS, amt=16)
                self.t[f"b_ttdma_{oc % 2}"] = SOS.n
                if th == 0:
                    if oc >= 4:
                        drive(pregen, 1)
                else:
                    drive(cj03, 2)
            if th == 0:
                self.b_sp_th0 = SOS.n
                drive(pregen, 99)
            self.t["b_yfree"] = PE.n
        self.b_done_pe = PE.n
        self.b_done_sp = SOS.n
        drive(cj03, 999)
        if self.debug:
            self.op("gpsimd", lambda e: e.dma_start(
                out=self.dbgout["d_osp"], in_=self.out_spill),
                waits=[(SOS, self.b_done_sp)], inc=SDBG, amt=16)

    def emit_c_pre_gen(self):
        di = self.di
        CC, PE, DV, AC = self.CC, self.PE, self.DV, self.AC
        SCEB, SCS = self.S["ceb"], self.S["cs"]
        pb = self.pb
        tokv = self.b_tok
        dv_stg = 0
        for pc in range(4):
            w = [(PE, self.a2_done_pe)] if pc == 0 else [(DV, dv_stg)]
            self.op("sync", lambda e, pc=pc: e.dma_start(
                out=self.c_stg[0:64, :],
                in_=di["eb"][:, pc * 1024:(pc + 1) * 1024]),
                waits=w, inc=SCEB, amt=16)
            ld = SCEB.n
            self.op("vector", lambda e, pc=pc: e.tensor_copy(
                out=self.c_eb[0:64, pc * 1024:(pc + 1) * 1024],
                in_=self.c_stg[0:64, :]),
                waits=[(SCEB, ld)], inc=DV)
            dv_stg = DV.n
            if pc % 2 == 1:
                yield
        self.dv_eb = dv_stg
        for pc in range(2):
            self.op("sync", lambda e, pc=pc: e.dma_start(
                out=self.c_stg.rearrange("p (a c) -> p a c", a=16),
                in_=di["rwt"][pc * 2048:(pc + 1) * 2048, :].rearrange(
                    "(a p) c -> p a c", p=128)),
                waits=[(DV, dv_stg)], inc=SCEB, amt=16)
            ld = SCEB.n
            self.op("vector", lambda e, pc=pc: e.tensor_copy(
                out=self.c_rwt[:, pc * 1024:(pc + 1) * 1024],
                in_=self.c_stg[:, :]),
                waits=[(SCEB, ld)], inc=DV)
            dv_stg = DV.n
        self.dv_rwt = dv_stg
        yield
        self.op("sync", lambda e: e.dma_start(
            out=self.csc[0:1, 0:64], in_=di["rbias"]),
            waits=[(self.S["sq"], self.g("a2_ssqr_sp")), (DV, dv_stg)],
            inc=SCEB, amt=16)
        ld_rb = SCEB.n
        self.op("vector", lambda e: e.tensor_copy(
            out=self.rbias_b[:], in_=self.csc[0:1, 0:64]),
            waits=[(SCEB, ld_rb)], inc=DV)
        self.dv_rb = DV.n

        def ssq_dma(e):
            return e.dma_start(out=self.ssqrow8.ap()[0:1, :],
                               in_=self.ssq_out[0:1, bass.ds(tokv["v"], 1024)])
        self.op("gpsimd", ssq_dma, waits=[(CC, self.cc_ssq)], inc=SCS, amt=16)
        sp_ssq = SCS.n
        yield
        for jj in range(8):
            self.op("tensor", lambda e, jj=jj: e.matmul(
                out=pb[4].ap()[:, 64 + jj:65 + jj],
                lhsT=self.ssqrow8.ap()[0:1, jj * 128:(jj + 1) * 128],
                rhs=self.onesrow[0:1, 0:1],
                start=True, stop=True),
                waits=([(SCS, sp_ssq), (AC, self.g("a2_ab_free_0")),
                        (DV, self.g("a2_ssqev_dv"))] if jj == 0 else []),
                inc=(PE if jj == 7 else None))
        pe_ssqt = PE.n
        self.op("scalar", lambda e: e.activation(
            out=self.csc2[:, 0:8],
            in_=pb[4].ap()[:, 64:72], func=AF.Sqrt, scale=1.0 / 8192.0,
            bias=self.epsb[:]),
            waits=[(PE, pe_ssqt)], inc=AC)
        self.ac_sr = AC.n
        yield
        self.op("vector", lambda e: e.reciprocal(
            out=self.irms[:], in_=self.csc2[:, 0:8]),
            waits=[(AC, self.ac_sr)], inc=DV)
        self.dv_irms = DV.n

    def emit_c_j_gen(self, j):
        di = self.di
        CC, PE, DV, AC = self.CC, self.PE, self.DV, self.AC
        SCX, SOUT = self.S["cx"], self.S["out"]
        pb = self.pb
        bsp = self.b_sp_th0 if j < 4 else self.b_done_sp
        self.op("gpsimd", lambda e, j=j: e.dma_start(
            out=self.c_x, in_=di["x_own"][j * 128:(j + 1) * 128, :]),
            waits=[(self.S["os"], bsp), (SOUT, self.g("c_outsp")),
                   (DV, self.g("c_mixdone"))], inc=SCX, amt=16)
        self.op("gpsimd", lambda e, j=j: e.dma_start(
            out=self.c_out, in_=self.out_spill[j * 128:(j + 1) * 128, :]),
            inc=SCX, amt=16)
        ld_j = SCX.n
        self.op("vector", lambda e, j=j: e.tensor_scalar(
            out=self.c_t1, in0=self.c_x,
            scalar1=self.mcol[:, j:j + 1], scalar2=self.invcol[:, j:j + 1],
            op0=ALU.subtract, op1=ALU.mult),
            waits=[(SCX, ld_j), (DV, self.dv_irms), (DV, self.dv_rb),
                   (PE, self.g("c_lg_pe"))], inc=DV)
        self.op("vector", lambda e, j=j: e.scalar_tensor_tensor(
            out=self.c_out, in0=self.c_out, scalar=self.irms[:, j:j + 1],
            in1=self.c_x, op0=ALU.mult, op1=ALU.add), inc=DV)
        self.op("vector", lambda e: e.tensor_tensor(
            out=self.c_x2, in0=self.c_out, in1=self.c_t1, op=ALU.add),
            waits=[(DV, DV.n)], inc=DV)
        dv_x2 = DV.n
        yield
        self.op("scalar", lambda e: e.activation(
            out=self.c_dump, in_=self.c_x2, func=AF.Copy,
            accum_out=self.statsc[:, 0:1]),
            waits=[(DV, dv_x2), (DV, self.g("c_stats_done")),
                   (AC, AC.n)], inc=AC)
        self.op("scalar", lambda e: e.activation(
            out=self.c_dump, in_=self.c_x2, func=AF.Square,
            accum_out=self.statsc[:, 2:3]), waits=[(AC, AC.n)], inc=AC)
        ac_acc2 = AC.n
        self.op("vector", lambda e: e.tensor_scalar(
            out=self.statsc[:, 1:2], in0=self.statsc[:, 0:1],
            scalar1=1.0 / DM, scalar2=None, op0=ALU.mult),
            waits=[(AC, ac_acc2)], inc=DV)
        self.op("vector", lambda e: e.tensor_scalar(
            out=self.statsc[:, 3:4], in0=self.statsc[:, 2:3],
            scalar1=1.0 / DM, scalar2=None, op0=ALU.mult),
            waits=[(DV, DV.n)], inc=DV)
        self.op("vector", lambda e: e.tensor_tensor(
            out=self.statsc[:, 4:5], in0=self.statsc[:, 1:2],
            in1=self.statsc[:, 1:2], op=ALU.mult),
            waits=[(DV, DV.n)], inc=DV)
        self.op("vector", lambda e: e.tensor_tensor(
            out=self.statsc[:, 5:6], in0=self.statsc[:, 3:4],
            in1=self.statsc[:, 4:5], op=ALU.subtract),
            waits=[(DV, DV.n)], inc=DV)
        dv_v2 = DV.n
        self.t["c_stats_done"] = dv_v2
        self.op("scalar", lambda e: e.activation(
            out=self.statsc[:, 6:7], in_=self.statsc[:, 5:6],
            func=AF.Sqrt, bias=self.epsb[:]), waits=[(DV, dv_v2)], inc=AC)
        ac_sd2 = AC.n
        yield
        self.op("vector", lambda e: e.reciprocal(
            out=self.statsc[:, 7:8], in_=self.statsc[:, 6:7]),
            waits=[(AC, ac_sd2)], inc=DV)
        dv_i2 = DV.n
        self.op("vector", lambda e: e.tensor_scalar(
            out=self.c_ln2f, in0=self.c_x2,
            scalar1=self.statsc[:, 1:2], scalar2=self.statsc[:, 7:8],
            op0=ALU.subtract, op1=ALU.mult), waits=[(DV, dv_i2)], inc=DV)
        self.op("vector", lambda e: e.tensor_copy(
            out=self.c_ln2b[:], in_=self.c_ln2f),
            waits=[(DV, DV.n), (AC, ac_acc2)], inc=DV)
        dv_ln2 = DV.n
        yield
        for half in range(2):
            for b in range(4 * half, 4 * half + 4):
                hh = b % 2
                for t in range(4):
                    fc = b * 4 + t
                    self.op("tensor", lambda e, fc=fc, t=t, hh=hh: e.transpose(
                        out=self.pb6.ap()[:, hh * 512 + t * 128:
                                          hh * 512 + (t + 1) * 128],
                        in_=self.c_ln2b[:, fc * 128:(fc + 1) * 128],
                        identity=self.identb[:]),
                        waits=([(DV, dv_ln2), (DV, self.g("c_trfree_0")),
                                (DV, self.g("c_trfree_1"))]
                               if t == 0 else []),
                        inc=(PE if t == 3 else None))
                pe_tr = PE.n
                self.op("vector", lambda e, b=b, hh=hh: e.tensor_copy(
                    out=self.c_ln2t[:, b * 512:(b + 1) * 512],
                    in_=self.pb6.ap()[:, hh * 512:(hh + 1) * 512]),
                    waits=[(PE, pe_tr)], inc=DV)
                self.t[f"c_trfree_{hh}"] = DV.n
            yield
        dv_lt = DV.n
        for ic in range(NIC):
            w = []
            if ic == 0:
                w = [(DV, dv_lt), (DV, self.g("c_lgfree"))]
                if j == 0:
                    w.append((AC, self.ac_sr))
            self.op("tensor", lambda e, ic=ic: e.matmul(
                out=pb[4].ap()[:, 0:64],
                lhsT=self.c_ln2t[:, ic * 128:(ic + 1) * 128],
                rhs=self.c_rwt[:, ic * 64:(ic + 1) * 64],
                start=(ic == 0), stop=False), waits=w)
        self.op("tensor", lambda e: e.matmul(
            out=pb[4].ap()[:, 0:64], lhsT=self.ones_bf_row(),
            rhs=self.rbias_b[:], start=False, stop=True), inc=PE)
        pe_lg = PE.n
        self.t["c_lg_pe"] = pe_lg
        CS = self.csc
        self.op("vector", lambda e: e.tensor_copy(out=CS[:, 0:64],
                                                  in_=pb[4].ap()[:, 0:64]),
                waits=[(PE, pe_lg)], inc=DV)
        self.op("vector", lambda e: e.tensor_copy(out=CS[:, 64:128],
                                                  in_=CS[:, 0:64]), inc=DV)
        dv_lg = DV.n
        self.t["c_lgfree"] = dv_lg
        self.op("vector", lambda e: e.reduce_max(
            out=self.csc2[:, 0:1], in_=CS[:, 64:128], axis=AXX,
            negate=True), inc=DV)
        yield
        for it in range(7):
            self.op("vector", lambda e: e.reduce_max(
                out=self.wsum[:], in_=CS[:, 64:128], axis=AXX), inc=DV)
            dv_rm = DV.n
            self.op("vector", lambda e: e.tensor_scalar(
                out=CS[:, 128:192], in0=CS[:, 64:128],
                scalar1=self.wsum[:], scalar2=None, op0=ALU.is_equal),
                waits=[(DV, dv_rm)], inc=DV)
            dv_eq = DV.n
            self.op("vector", lambda e: e.scalar_tensor_tensor(
                out=CS[:, 64:128], in0=CS[:, 128:192], scalar=NEG,
                in1=CS[:, 64:128], op0=ALU.mult, op1=ALU.add),
                waits=[(DV, dv_eq)], inc=DV)
            if it == 3:
                yield
        self.op("vector", lambda e: e.reduce_max(
            out=self.csc2[:, 1:2], in_=CS[:, 64:128], axis=AXX), inc=DV)
        dv_t8 = DV.n
        yield
        self.op("vector", lambda e: e.tensor_scalar(
            out=CS[:, 128:192], in0=CS[:, 0:64],
            scalar1=self.csc2[:, 1:2], scalar2=None, op0=ALU.is_ge),
            waits=[(DV, dv_t8)], inc=DV)
        self.op("scalar", lambda e: e.activation(
            out=CS[:, 192:256], in_=CS[:, 0:64], func=AF.Exp,
            bias=self.csc2[:, 0:1]), waits=[(DV, dv_t8)], inc=AC)
        ac_e = AC.n
        self.op("vector", lambda e: e.reduce_sum(
            out=self.csc2[:, 2:3], in_=CS[:, 192:256], axis=AXX),
            waits=[(AC, ac_e)], inc=DV)
        dv_den = DV.n
        self.op("vector", lambda e: e.tensor_tensor(
            out=CS[:, 256:320], in0=CS[:, 192:256], in1=CS[:, 128:192],
            op=ALU.mult), waits=[(DV, DV.n)], inc=DV)
        self.op("vector", lambda e: e.reduce_sum(
            out=self.csc2[:, 3:4], in_=CS[:, 256:320], axis=AXX), inc=DV)
        dv_sk = DV.n
        self.op("vector", lambda e: e.reciprocal(
            out=self.csc2[:, 4:5], in_=self.csc2[:, 2:3]),
            waits=[(DV, dv_den)], inc=DV)
        dv_rd = DV.n
        self.op("vector", lambda e: e.tensor_tensor(
            out=self.wsum[:], in0=self.csc2[:, 3:4], in1=self.csc2[:, 4:5],
            op=ALU.mult), waits=[(DV, dv_sk), (DV, dv_rd)], inc=DV)
        dv_ws = DV.n
        self.op("vector", lambda e: e.tensor_scalar(
            out=CS[:, 320:384], in0=CS[:, 256:320],
            scalar1=self.csc2[:, 4:5], scalar2=None, op0=ALU.mult),
            waits=[(DV, dv_rd)], inc=DV)
        dv_up = DV.n
        self.op("vector", lambda e: e.tensor_copy(
            out=self.ubf[:], in_=CS[:, 320:384]),
            waits=[(DV, dv_up), (DV, dv_ws)], inc=DV)
        dv_u = DV.n
        yield
        self.op("tensor", lambda e: e.transpose(
            out=self.pb6.ap()[0:64, 0:128], in_=self.ubf[:],
            identity=self.identb[:]),
            waits=[(DV, dv_u), (DV, self.g("c_trfree_0")),
                   (DV, self.g("c_trfree_1"))], inc=PE)
        pe_ut = PE.n
        self.op("vector", lambda e: e.tensor_copy(
            out=self.ubT[:], in_=self.pb6.ap()[0:64, 0:128]),
            waits=[(PE, pe_ut)], inc=DV)
        dv_ut = DV.n
        self.t["c_trfree_0"] = dv_ut
        for ob in range(8):
            bmbank = pb[5] if ob % 2 == 0 else self.pb7
            w = [(DV, dv_ut), (DV, self.dv_eb),
                 (DV, self.g(f"c_bmfree_{ob % 2}"))]
            if j == 0 and ob < 2:
                w += [(AC, self.g("a2_ab_free_1")), (DV, self.g("a2_yev_dv"))]
            self.op("tensor", lambda e, ob=ob, bmbank=bmbank: e.matmul(
                out=bmbank.ap()[:], lhsT=self.ubT[:],
                rhs=self.c_eb[0:64, ob * 512:(ob + 1) * 512],
                start=True, stop=True),
                waits=w, inc=PE)
            pe_bm = PE.n
            self.op("vector", lambda e, ob=ob, bmbank=bmbank: e.scalar_tensor_tensor(
                out=self.c_t1[:, ob * 512:(ob + 1) * 512],
                in0=self.c_ln2f[:, ob * 512:(ob + 1) * 512],
                scalar=self.wsum[:], in1=bmbank.ap()[:],
                op0=ALU.mult, op1=ALU.add),
                waits=[(PE, pe_bm), (DV, dv_ws)], inc=DV)
            self.t[f"c_bmfree_{ob % 2}"] = DV.n
            self.op("vector", lambda e, ob=ob: e.tensor_tensor(
                out=self.c_outst[:, ob * 512:(ob + 1) * 512],
                in0=self.c_t1[:, ob * 512:(ob + 1) * 512],
                in1=self.c_x2[:, ob * 512:(ob + 1) * 512], op=ALU.add),
                waits=[(DV, DV.n)], inc=DV)
            if ob == 3:
                yield
        dv_o = DV.n
        self.t["c_mixdone"] = dv_o
        self.op("gpsimd", lambda e, j=j: e.dma_start(
            out=self.out[j * 128:(j + 1) * 128, :], in_=self.c_outst),
            waits=[(DV, dv_o)], inc=SOUT, amt=16)
        self.t["c_outsp"] = SOUT.n
        yield

    def emit_c_rest(self):
        for j in range(4, 8):
            for _ in self.emit_c_j_gen(j):
                pass
        final = self.S["out"].n
        self.ops["gpsimd"].append(
            lambda e, final=final: e.wait_ge(self.S["out"].h, final))

    def ones_bf_row(self):
        return self.trimask.ap()[0:1, :]


_CACHE = {}


def _build(debug=False):
    key = ("nc", debug)
    if key not in _CACHE:
        _CACHE[key] = KB(debug=debug).build()
    return _CACHE[key]


def _rep3(v):
    out = np.zeros((P, 1), np.float32)
    out[0:NH, 0] = v
    out[32:32 + NH, 0] = v
    out[64:64 + NH, 0] = v
    return out


def _stage(inputs, k):
    f32 = np.float32
    bf = ml_dtypes.bfloat16
    x = np.asarray(inputs["L0_attn_hidden_states"], f32).reshape(L, DM)
    W = np.asarray(inputs["L0_attn_in_proj_w"], f32)
    cw = np.asarray(inputs["L0_attn_conv_w"], f32)
    cb = np.asarray(inputs["L0_attn_conv_b"], f32)
    dtb = np.asarray(inputs["L0_attn_dt_bias"], f32)
    alog = np.asarray(inputs["L0_attn_A_log"], f32)
    dv = np.asarray(inputs["L0_attn_D"], f32)
    nw = np.asarray(inputs["L0_attn_norm_w"], f32)
    wo = np.asarray(inputs["L0_attn_out_proj_w"], f32)
    rw = np.asarray(inputs["L0_ffn_router_weight"], f32)
    rb = np.asarray(inputs["L0_ffn_router_bias"], f32)
    eb = np.asarray(inputs["L0_ffn_expert_bias"], f32)

    zrows = W[1024 * k:1024 * (k + 1)]
    dtrows = W[18432 + 16 * k:18432 + 16 * (k + 1)]
    xrows = W[8192 + 1024 * k:8192 + 1024 * (k + 1)]
    brows = W[16384 + 128 * k:16384 + 128 * (k + 1)]
    crows = W[17408 + 128 * k:17408 + 128 * (k + 1)]
    wa1 = np.concatenate([zrows, dtrows], axis=0)
    wa2 = np.concatenate([xrows, brows, crows], axis=0)
    cwS = np.concatenate([cw[1024 * k:1024 * (k + 1)],
                          cw[8192 + 128 * k:8192 + 128 * (k + 1)],
                          cw[9216 + 128 * k:9216 + 128 * (k + 1)]], axis=0)
    cbS = np.concatenate([cb[1024 * k:1024 * (k + 1)],
                          cb[8192 + 128 * k:8192 + 128 * (k + 1)],
                          cb[9216 + 128 * k:9216 + 128 * (k + 1)]], axis=0)
    d = {
        "x_own": np.ascontiguousarray(x[1024 * k:1024 * (k + 1)]),
        "wa1t": np.ascontiguousarray(wa1.T),
        "wa2t": np.ascontiguousarray(wa2.T),
        "wot": np.ascontiguousarray(wo[:, 1024 * k:1024 * (k + 1)].T),
        "cw_pg": np.ascontiguousarray(
            cwS.reshape(10, 128, 4).transpose(1, 0, 2).reshape(128, 40)),
        "cb_pg": np.ascontiguousarray(cbS.reshape(10, 128).T),
        "dtb": _rep3(dtb[16 * k:16 * (k + 1)]),
        "alog": _rep3(alog[16 * k:16 * (k + 1)]),
        "dvec_row": dv[16 * k:16 * (k + 1)].reshape(1, NH),
        "normw_pg": np.ascontiguousarray(
            nw[1024 * k:1024 * (k + 1)].reshape(8, 128).T),
        "rwt": np.ascontiguousarray(rw.T),
        "rbias": rb.reshape(1, 64),
        "eb": eb,
        "identf": np.eye(P, dtype=f32),
        "identb": np.eye(P, dtype=f32).astype(bf),
        "onesrow_f": np.ones((1, P), f32),
        "onescol_f": np.ones((P, 1), f32),
        "trimask": np.triu(np.ones((P, P), f32)).astype(bf),
        "tok0": np.array([[1024 * k]], dtype=np.int32),
    }
    for key in d:
        if d[key].dtype == np.float64:
            d[key] = d[key].astype(f32)
    return d


def kernel(**inputs):
    nc = _build(debug=False)
    in_maps = [_stage(inputs, k) for k in range(NCORES)]
    res = run_bass_kernel_spmd(nc, in_maps, core_ids=list(range(NCORES)))
    out = np.concatenate([np.asarray(res.results[k]["out"], np.float32)
                          for k in range(NCORES)], axis=0)
    return out.reshape(1, L, DM)


# version 50
# speedup vs baseline: 1.1377x; 1.1377x over previous
import contextlib
import numpy as np
import ml_dtypes

import concourse.bass as bass
import concourse.mybir as mybir
from concourse.bass_utils import run_bass_kernel_spmd

F32 = mybir.dt.float32
BF16 = mybir.dt.bfloat16
I32 = mybir.dt.int32
AF = mybir.ActivationFunctionType
ALU = mybir.AluOpType
AXX = mybir.AxisListType.X

NCORES = 8
P = 128
DM = 4096
L = 8192
DIC = 1024
NH = 16
HD = 64
TC = 512
NTC = L // TC
CH = 128
NCHTC = TC // CH
HF = 256
WA1C = DIC + NH
WA2C = DIC + 256
NIC = DM // P
RG = [list(range(NCORES))]
NEG = -1e30


class Sem:
    def __init__(self, ctx, nc, name):
        self.h = ctx.enter_context(nc.semaphore(name))
        self.n = 0


class KB:

    def __init__(self, debug=False):
        self.debug = debug
        nc = bass.Bass("TRN2", target_bir_lowering=False, debug=False)
        self.nc = nc
        self.ops = {k: [] for k in ("sync", "tensor", "vector", "scalar", "gpsimd")}
        di = {}

        def ein(name, shape, dt=F32):
            di[name] = nc.dram_tensor(name, shape, dt, kind="ExternalInput").ap()

        ein("x_own", [1024, DM])
        ein("wa1t", [DM, WA1C])
        ein("wa2t", [DM, WA2C])
        ein("wot", [DIC, DM])
        ein("cw_pg", [P, 40])
        ein("cb_pg", [P, 10])
        ein("dtb", [P, 1])
        ein("alog", [P, 1])
        ein("dvec_row", [1, NH])
        ein("normw_pg", [P, 8])
        ein("rwt", [DM, 64])
        ein("rbias", [1, 64])
        ein("eb", [64, DM])
        ein("identf", [P, P])
        ein("identb", [P, P], BF16)
        ein("onesrow_f", [1, P])
        ein("onescol_f", [P, 1])
        ein("trimask", [P, P], BF16)
        ein("tok0", [1, 1], I32)
        self.di = di
        self.out = nc.dram_tensor("out", [1024, DM], F32, kind="ExternalOutput").ap()

        def idr(name, shape, dt=F32):
            return nc.dram_tensor(name, shape, dt, kind="Internal").ap()

        def idrs(name, shape, dt=F32):
            return nc.dram_tensor(name, shape, dt, kind="Internal",
                                  addr_space="Shared").ap()

        self.lnxt_in = [idr(f"lnxt_in{h}", [DM, TC], BF16) for h in range(2)]
        self.lnxt_all = [idrs(f"lnxt_all{h}", [NCORES * DM, TC], BF16)
                         for h in range(2)]
        self.wout_in = [idr(f"wout_in{q}", [DIC, 1024], BF16) for q in range(4)]
        self.wout_all = [idrs(f"wout_all{q}", [NCORES * DIC, 1024], BF16)
                         for q in range(4)]
        self.z_spill = idr("z_spill", [DIC, L], BF16)
        self.dt_spill = idr("dt_spill", [NH, L])
        self.y_blk = idr("y_blk", [NTC * DIC, TC], BF16)
        self.y_allb = idrs("y_allb", [NTC * NCORES * DIC, TC], BF16)
        self.ssq_in = idr("ssq_in", [1, L])
        self.ssq_out = idrs("ssq_out", [1, L])
        self.combo_dr = idr("combo_dr", [NH, TC])
        self.combo_drb = idr("combo_drb", [2 * NH, TC], BF16)
        self.wa2bf = idr("wa2bf", [DM, WA2C], BF16)
        self.out_spill = idr("out_spill", [1024, DM])
        self.dbgout = {}
        if debug:
            for name, shape, dt in [
                ("d_lnxt", [NCORES * DM, 1024], BF16),
                ("d_stats", [P, 16], F32),
                ("d_x7", [P, DM], F32),
                ("d_st2", [P, 8], F32),
                ("d_wbuf", [P, NIC * WA1C], BF16),
                ("d_combo", [P, TC], F32),
                ("d_ctok", [P, TC], F32),
                ("d_xdt", [P, 4096], BF16),
                ("d_xdtd", [P, 4096], BF16),
                ("d_ytok", [P, 4096], BF16),
                ("d_sbf", [P, 4096], BF16),
                ("d_ln7", [P, DM], BF16),
                ("d_z", [DIC, L], BF16),
                ("d_dt", [NH, L], F32),
                ("d_y", [NTC * DIC, TC], BF16),
                ("d_ssq", [1, L], F32),
                ("d_osp", [1024, DM], F32),
            ]:
                self.dbgout[name] = nc.dram_tensor(name, shape, dt,
                                                   kind="ExternalOutput").ap()

    def op(self, engine, fn, waits=(), inc=None, amt=1):
        if inc is not None:
            inc.n += amt

        def g(eng, fn=fn, waits=tuple(waits), inc=inc, amt=amt):
            for (s, v) in waits:
                if v > 0:
                    eng.wait_ge(s.h, v)
            ins = fn(eng)
            if inc is not None:
                ins.then_inc(inc.h, amt)
        self.ops[engine].append(g)

    def build(self):
        nc = self.nc
        with contextlib.ExitStack() as ctx:
            self.alloc(ctx)
            self.emit_all()
            blk = ctx.enter_context(nc.Block())
            blk.sync(lambda e: [f(e) for f in self.ops["sync"]])
            blk.tensor(lambda e: [f(e) for f in self.ops["tensor"]])
            blk.vector(lambda e: [f(e) for f in self.ops["vector"]])
            blk.scalar(lambda e: [f(e) for f in self.ops["scalar"]])
            blk.gpsimd(lambda e: [f(e) for f in self.ops["gpsimd"]])
        return nc

    def alloc(self, ctx):
        nc = self.nc
        sb = lambda name, shape, dt=F32: ctx.enter_context(nc.sbuf_tensor(name, shape, dt))
        ps = lambda name, shape, dt=F32: ctx.enter_context(nc.psum_tensor(name, shape, dt))
        self.wbuf = sb("wbuf", [P, NIC * WA2C], BF16)
        self.lnx = sb("lnx", [P, 16384], BF16)
        self.wstage = sb("wstage", [P, 2560], F32)
        self.arena = sb("arena", [P, 24576], BF16)
        a = self.arena.ap()
        self.xbcpre = a[:, 0:5150]
        self.xbcpost = a[:, 5152:10272]
        self.siluz = a[:, 10272:14368]
        self.yfm = a[:, 14368:18464]
        self.xdt = a[:, 18464:19488]
        self.xdtd = a[:, 19488:20512]
        self.ytok = a[:, 20512:21536]
        self.zstg = a[:, 21536:22048]
        self.xD = a[:, 22528:24576].bitcast(F32)
        self.a0_x = a[:, 0:8192].bitcast(F32)
        self.a0_ln = a[:, 8192:12288]
        self.a0_tr = a[:, 12288:13312]
        self.a0_wo = a[:, 13312:15360]
        self.a0_sq = a[:, 8192:12288].bitcast(F32)
        self.a0_xb = a[:, 15360:23552].bitcast(F32)
        self.lnxb0 = a[:, 0:8192]
        self.lnxb1 = a[:, 15360:23552]
        self.wa2st = a[:, 8192:10752]
        pass
        self.c_x = a[:, 0:8192].bitcast(F32)
        self.c_out = a[:, 8192:16384].bitcast(F32)
        self.c_ln2f = self.c_out
        self.c_t1 = a[:, 16384:24576].bitcast(F32)
        self.c_ln2b = a[:, 16384:20480]
        self.c_ln2t = a[:, 20480:24576]
        self.c_dump = a[:, 16384:20480]
        self.c_x2 = self.c_x
        self.c_outst = self.c_x
        wb = self.wbuf.ap()
        self.b_y = wb[:, 0:32768]
        self.c_eb = wb[:, 32768:36864]
        self.c_rwt = wb[:, 36864:38912]
        self.c_stg = wb[:, 38912:40960].bitcast(F32)
        ln = self.lnx.ap()
        self.b_w = ln[:, 0:16384]
        self.c_sq = ln[:, 0:8192].bitcast(F32)
        self.identf = sb("identf_s", [P, P])
        self.identb = sb("identb_s", [P, P], BF16)
        self.onesrow = sb("onesrow_s", [1, P])
        self.onescol = sb("onescol_s", [P, 1])
        self.trimask = sb("trimask_s", [P, P], BF16)
        self.cw = sb("cw_s", [P, 40])
        self.cb = sb("cb_s", [P, 10])
        self.dtbias = sb("dtb_s", [P, 1])
        self.alogc = sb("alog_s", [P, 1])
        self.apos = sb("apos_s", [P, 1])
        self.dvrow = sb("dvrow_s", [1, NH])
        self.dbc = sb("dbc_s", [P, NH])
        self.normw = sb("normw_s", [P, 8])
        self.dgw = sb("dgw_s", [P, 40 * P], BF16)
        self.mcol = sb("mcol_s", [P, 8])
        self.invcol = sb("invcol_s", [P, 8])
        self.irms = sb("irms_s", [P, 8])
        self.statsc = sb("statsc_s", [P, 8])
        self.sp2 = sb("sp2_s", [NH, TC])
        self.scanmask = sb("scanmask_s", [NH, TC])
        self.combo = sb("combo_s", [P, TC])
        self.ctok = sb("ctok_s", [P, TC])
        self.atok = sb("atok_s", [P, NH])
        self.dstok = sb("dstok_s", [P, NH])
        self.dsdiff = sb("dsdiff_s", [P, NH])
        self.expTb = sb("expTb_s", [P, NH])
        self.S_in = sb("S_in_s", [P, 1024])
        self.S_bf = sb("S_bf_s", [P, 1024], BF16)
        self.S_a = sb("S_a_s", [P, 1024])
        self.dtr = self.S_a.ap()[0:NH, 0:TC]
        self.Ebuf = sb("Ebuf_s", [P, 4 * P], BF16)
        self.Mbuf = sb("Mbuf_s", [P, 4 * P], BF16)
        self.Gm = sb("Gm_s", [P, P], BF16)
        self.Btok = sb("Btok_s", [P, P], BF16)
        self.yscr = sb("yscr_s", [P, 512])
        self.sp1 = self.yscr.ap()[0:NH, :]
        self.cumA = self.sp2.ap()[:]
        self.cumB = self.sp1
        self.sqs = self.S_a
        self.sqs_bf = self.S_a.ap()[:, 0:512].bitcast(BF16)
        self.ssqrow8 = self.S_a
        self.flatr = self.wstage
        self.b_sb2 = [self.wstage.ap()[:, 0:512],
                      self.wstage.ap()[:, 512:1024]]
        self.b_tt2 = [self.wstage.ap()[:, 1024:1536],
                      self.wstage.ap()[:, 1536:2048]]
        self.fbf = self.wstage.ap()[0:1, 0:2048].bitcast(BF16)
        self.chi = a[0:NH, 20512:21024]
        self.clo = a[0:NH, 21024:21536]
        self.zevac = None
        self.csc = sb("csc_s", [P, 512])
        self.zevac = self.sqs.ap()[:, 0:256].bitcast(BF16)
        self.zevac2 = self.sqs.ap()[:, 256:512].bitcast(BF16)
        self.dtev = self.csc.ap()[0:NH, :]
        self.ssqr = self.csc
        self.csc2 = sb("csc2_s", [P, 8])
        self.ubf = sb("ubf_s", [P, 64], BF16)
        self.ubT = sb("ubT_s", [64, P], BF16)
        self.rbias_b = sb("rbias_bs", [1, 64], BF16)
        self.wsum = sb("wsum_s", [P, 1])
        self.epsb = sb("epsb_s", [P, 1])
        self.pb = [ps(f"pb{i}", [P, 512]) for i in range(6)]
        self.pb6 = ps("pb6", [P, 1024], BF16)
        self.pb7 = ps("pb7", [P, 512])
        self.CC = Sem(ctx, nc, "CC")
        self.PE = Sem(ctx, nc, "PE")
        self.DV = Sem(ctx, nc, "DV")
        self.AC = Sem(ctx, nc, "AC")
        names = ["setup", "x", "wst0", "wst1", "lnx0", "lnx1", "dtr",
                 "z0", "z1", "bw0", "bw1", "ceb", "cx",
                 "gl", "wo0", "wo1", "zs0", "zs1", "ds0", "ds1",
                 "cm", "fl", "sq", "yi", "cs", "os", "out", "dbg", "by",
                 "wa2", "wa2l", "chl"]
        self.S = {n: Sem(ctx, nc, "S_" + n) for n in names}
        self.t = {}

    def g(self, key, default=0):
        return self.t.get(key, default)

    def emit_all(self):
        self.emit_setup()
        self.emit_a0()
        self.emit_a1()
        self.emit_a2()
        self.emit_b()
        self.emit_c_rest()

    def emit_setup(self):
        di, AC, DV, PE = self.di, self.AC, self.DV, self.PE
        SET = self.S["setup"]
        loads = [
            (self.identf, "identf"), (self.identb, "identb"),
            (self.onesrow, "onesrow_f"), (self.onescol, "onescol_f"),
            (self.trimask, "trimask"), (self.cw, "cw_pg"), (self.cb, "cb_pg"),
            (self.dtbias, "dtb"), (self.alogc, "alog"), (self.dvrow, "dvec_row"),
            (self.normw, "normw_pg"),
        ]
        for buf, name in loads:
            self.op("sync", lambda e, b=buf, n=name: e.dma_start(out=b[:], in_=di[n]),
                    inc=SET, amt=16)
        self.ld_setup = SET.n
        self.SET = SET
        self.op("vector", lambda e: e.memset(self.epsb[:], 1e-5), inc=DV)
        self.dv_eps = DV.n
        self.op("vector", lambda e: e.memset(self.scanmask.ap()[:], 1.0), inc=DV)
        self.op("vector", lambda e: e.memset(
            self.scanmask.ap().rearrange("h (c t) -> h c t", t=CH)[:, :, 0:1],
            0.0), waits=[(DV, DV.n)], inc=DV)
        self.op("scalar", lambda e: e.activation(out=self.apos[:], in_=self.alogc[:],
                                                 func=AF.Exp),
                waits=[(self.SET, self.ld_setup)], inc=AC)
        self.ac_apos = AC.n
        self.op("tensor", lambda e: e.matmul(
            out=self.pb[0].ap()[:, 0:NH], lhsT=self.onesrow[:], rhs=self.dvrow[:],
            start=True, stop=True), waits=[(self.SET, self.ld_setup)], inc=PE)
        pe_d = PE.n
        self.op("vector", lambda e: e.tensor_copy(out=self.dbc[:],
                                                  in_=self.pb[0].ap()[:, 0:NH]),
                waits=[(PE, pe_d)], inc=DV)
        self.dv_dbc = DV.n
        for g in range(10):
            for k in range(4):
                self.op("vector", lambda e, g=g, k=k: e.tensor_scalar(
                    out=self.dgw.ap()[:, (g * 4 + k) * P:(g * 4 + k + 1) * P],
                    in0=self.identb[:], scalar1=self.cw[:, g * 4 + k:g * 4 + k + 1],
                    scalar2=None, op0=ALU.mult),
                    waits=([(self.SET, self.ld_setup)] if g == 0 and k == 0 else []),
                    inc=(DV if (g == 9 and k == 3) else None))
        self.dv_diag = DV.n

    def emit_a0(self):
        di = self.di
        CC, PE, DV, AC = self.CC, self.PE, self.DV, self.AC
        SX, SGL, SDBG = self.S["x"], self.S["gl"], self.S["dbg"]
        self.wa1_ready = []
        for j in range(8):
            xb = self.a0_x if j % 2 == 0 else self.a0_xb
            self.op("sync", lambda e, j=j, xb=xb: e.dma_start(
                out=xb, in_=di["x_own"][j * 128:(j + 1) * 128, :]),
                waits=[(DV, self.g(f"a0_xfree_{j % 2}"))], inc=SX, amt=16)
            ld_x = SX.n
            self.op("scalar", lambda e, xb=xb: e.activation(
                out=self.a0_ln, in_=xb, func=AF.Copy,
                accum_out=self.statsc[:, 0:1]),
                waits=[(SX, ld_x), (DV, self.g("a0_stats_done")),
                       (PE, self.g("a0_trpe")), (AC, AC.n)], inc=AC)
            self.op("scalar", lambda e, xb=xb: e.activation(
                out=self.a0_ln, in_=xb, func=AF.Square,
                accum_out=self.statsc[:, 2:3]), waits=[(AC, AC.n)], inc=AC)
            ac_acc = AC.n
            self.op("vector", lambda e, j=j: e.tensor_scalar(
                out=self.mcol[:, j:j + 1], in0=self.statsc[:, 0:1],
                scalar1=1.0 / DM, scalar2=None, op0=ALU.mult),
                waits=[(AC, ac_acc)], inc=DV)
            self.op("vector", lambda e: e.tensor_scalar(
                out=self.statsc[:, 3:4], in0=self.statsc[:, 2:3],
                scalar1=1.0 / DM, scalar2=None, op0=ALU.mult),
                waits=[(DV, DV.n)], inc=DV)
            self.op("vector", lambda e, j=j: e.tensor_tensor(
                out=self.statsc[:, 4:5], in0=self.mcol[:, j:j + 1],
                in1=self.mcol[:, j:j + 1], op=ALU.mult),
                waits=[(DV, DV.n)], inc=DV)
            self.op("vector", lambda e: e.tensor_tensor(
                out=self.statsc[:, 5:6], in0=self.statsc[:, 3:4],
                in1=self.statsc[:, 4:5], op=ALU.subtract),
                waits=[(DV, DV.n)], inc=DV)
            dv_var = DV.n
            self.t["a0_stats_done"] = dv_var
            self.op("scalar", lambda e: e.activation(
                out=self.statsc[:, 6:7], in_=self.statsc[:, 5:6],
                func=AF.Sqrt, bias=self.epsb[:]),
                waits=[(DV, dv_var), (DV, self.dv_eps)], inc=AC)
            ac_sd = AC.n
            self.op("vector", lambda e, j=j: e.reciprocal(
                out=self.invcol[:, j:j + 1], in_=self.statsc[:, 6:7]),
                waits=[(AC, ac_sd)], inc=DV)
            dv_inv = DV.n
            self.op("vector", lambda e, j=j, xb=xb: e.tensor_scalar(
                out=self.a0_ln, in0=xb,
                scalar1=self.mcol[:, j:j + 1], scalar2=self.invcol[:, j:j + 1],
                op0=ALU.subtract, op1=ALU.mult),
                waits=[(DV, dv_inv), (PE, self.g("a0_trpe"))], inc=DV)
            dv_ln = DV.n
            self.t[f"a0_xfree_{j % 2}"] = dv_ln
            for b in range(8):
                half = b % 2
                for t in range(4):
                    fc = b * 4 + t
                    self.op("tensor", lambda e, fc=fc, t=t, half=half: e.transpose(
                        out=self.pb6.ap()[:, half * 512 + t * 128:
                                          half * 512 + (t + 1) * 128],
                        in_=self.a0_ln[:, fc * 128:(fc + 1) * 128],
                        identity=self.identb[:]),
                        waits=([(DV, dv_ln), (DV, self.g("a0_bf0")),
                                (DV, self.g("a0_bf1"))]
                               if t == 0 else []),
                        inc=(PE if t == 3 else None))
                pe_tr = PE.n
                self.t["a0_trpe"] = pe_tr
                self.op("vector", lambda e, half=half: e.tensor_copy(
                    out=self.a0_tr[:, half * 512:(half + 1) * 512],
                    in_=self.pb6.ap()[:, half * 512:(half + 1) * 512]),
                    waits=[(PE, pe_tr), (SGL, self.g("a0_trsp"))], inc=DV)
                dv_ev = DV.n
                self.t[f"a0_bf{half}"] = dv_ev
                self.op("gpsimd", lambda e, j=j, b=b, half=half: e.dma_start(
                    out=self.lnxt_in[j // 4][b * 512:(b + 1) * 512,
                                             (j % 4) * 128:(j % 4 + 1) * 128
                                             ].rearrange("(a p) t -> p a t", p=128),
                    in_=self.a0_tr[:, half * 512:(half + 1) * 512].rearrange(
                        "p (a t) -> p a t", a=4)),
                    waits=[(DV, dv_ev)], inc=SGL, amt=16)
                self.t["a0_trsp"] = SGL.n
            if j == 3 or j == 7:
                hh = j // 4
                self.op("gpsimd", lambda e, hh=hh: e.collective_compute(
                    "AllGather", ALU.bypass, replica_groups=RG,
                    ins=[self.lnxt_in[hh].opt()], outs=[self.lnxt_all[hh].opt()]),
                    waits=[(SGL, SGL.n)], inc=CC)
                if hh == 0:
                    self.cc_lnxt0 = CC.n
                else:
                    self.cc_lnxt1 = CC.n
            for ic in range(4 * j, 4 * j + 4):
                self.emit_wa1_piece(ic)
        if self.debug:
            self.op("gpsimd", lambda e: e.dma_start(
                out=self.dbgout["d_x7"], in_=self.a0_xb),
                waits=[(DV, DV.n)], inc=SDBG, amt=16)
            self.op("gpsimd", lambda e: e.dma_start(
                out=self.dbgout["d_ln7"], in_=self.a0_ln), inc=SDBG, amt=16)
            self.op("gpsimd", lambda e: e.dma_start(
                out=self.dbgout["d_stats"][:, 0:8], in_=self.mcol[:]), inc=SDBG, amt=16)
            self.op("gpsimd", lambda e: e.dma_start(
                out=self.dbgout["d_stats"][:, 8:16], in_=self.invcol[:]),
                inc=SDBG, amt=16)
            self.op("gpsimd", lambda e: e.dma_start(
                out=self.dbgout["d_st2"], in_=self.statsc[:]), inc=SDBG, amt=16)
        self.sp_lnxt = SGL.n
        self.a0_last_sq_dv = DV.n

    def emit_wconv(self, wname, ncols, first_waits):
        di, DV = self.di, self.DV
        ready = []
        for ic in range(NIC):
            slot = ic % 2
            SW = self.S[f"wst{slot}"]
            ws = self.wstage.ap()[:, slot * 1280: slot * 1280 + ncols]
            w = [(DV, self.g(f"ws_free_{slot}"))]
            if ic < 2:
                w += first_waits
            self.op("sync", lambda e, ic=ic, ws=ws: e.dma_start(
                out=ws, in_=di[wname][ic * 128:(ic + 1) * 128, :]),
                waits=w, inc=SW, amt=16)
            ld_w = SW.n
            dst = self.wbuf.ap()[:, ic * ncols:(ic + 1) * ncols]
            self.op("vector", lambda e, ws=ws, dst=dst: e.tensor_copy(
                out=dst, in_=ws), waits=[(SW, ld_w)] + (first_waits if ic < 2 else []),
                inc=DV)
            self.t[f"ws_free_{slot}"] = DV.n
            ready.append(DV.n)
        return ready

    def emit_wa1_piece(self, ic):
        di, AC = self.di, self.AC
        slot = ic % 2
        SW = self.S[f"wst{slot}"]
        ws = self.wstage.ap()[:, slot * 1280: slot * 1280 + WA1C]
        self.op("sync", lambda e, ws=ws, ic=ic: e.dma_start(
            out=ws, in_=di["wa1t"][ic * 128:(ic + 1) * 128, :]),
            waits=[(AC, self.g(f"wsA_free_{slot}"))], inc=SW, amt=16)
        ld_w = SW.n
        dst = self.wbuf.ap()[:, ic * WA1C:(ic + 1) * WA1C]
        self.op("scalar", lambda e, ws=ws, dst=dst: e.activation(
            out=dst, in_=ws, func=AF.Copy), waits=[(SW, ld_w)], inc=AC)
        self.t[f"wsA_free_{slot}"] = AC.n
        self.wa1_ready.append(AC.n)

    def emit_wout_piece(self, oc, hh, slot, extra_waits=()):
        di, DV = self.di, self.DV
        SW = self.S[f"wst{slot}"]
        SWO = self.S[f"wo{slot}"]
        ws = self.wstage.ap()[:, slot * 1280:slot * 1280 + 1024]
        self.op("sync", lambda e, oc=oc, hh=hh, ws=ws: e.dma_start(
            out=ws, in_=di["wot"][oc * 128:(oc + 1) * 128,
                                  hh * 1024:(hh + 1) * 1024]),
            waits=[(DV, self.g(f"ws_free_{slot}"))] + list(extra_waits),
            inc=SW, amt=16)
        ld_w = SW.n
        ob = self.a0_wo[:, slot * 1024:(slot + 1) * 1024]
        self.op("vector", lambda e, ws=ws, ob=ob, oc=oc: e.tensor_scalar(
            out=ob, in0=ws, scalar1=self.normw[:, oc:oc + 1],
            scalar2=None, op0=ALU.mult),
            waits=[(SW, ld_w), (SWO, self.g(f"wo_sp_{slot}"))], inc=DV)
        dv_w = DV.n
        self.t[f"ws_free_{slot}"] = dv_w
        self.op("gpsimd", lambda e, oc=oc, hh=hh, ob=ob: e.dma_start(
            out=self.wout_in[hh][oc * 128:(oc + 1) * 128, :], in_=ob),
            waits=[(DV, dv_w)], inc=SWO, amt=16)
        self.t[f"wo_sp_{slot}"] = SWO.n

    def emit_wa2_piece(self, ic):
        di, DV = self.di, self.DV
        slot = ic % 2
        SW = self.S[f"wst{slot}"]
        SWA = self.S["wa2"] if slot == 0 else self.S["cm"]
        ws = self.wstage.ap()[:, slot * 1280:slot * 1280 + WA2C]
        self.op("sync", lambda e, ic=ic, ws=ws: e.dma_start(
            out=ws, in_=di["wa2t"][ic * 128:(ic + 1) * 128, :]),
            waits=[(DV, self.g(f"ws_free_{slot}"))], inc=SW, amt=16)
        ld_w = SW.n
        ob = self.wa2st[:, slot * 1280:(slot + 1) * 1280]
        self.op("vector", lambda e, ws=ws, ob=ob: e.tensor_copy(out=ob, in_=ws),
                waits=[(SW, ld_w), (SWA, self.g(f"wa2_sp_{slot}"))], inc=DV)
        dv_w = DV.n
        self.t[f"ws_free_{slot}"] = dv_w
        self.op("gpsimd", lambda e, ic=ic, ob=ob: e.dma_start(
            out=self.wa2bf[ic * 128:(ic + 1) * 128, :], in_=ob),
            waits=[(DV, dv_w)], inc=SWA, amt=16)
        self.t[f"wa2_sp_{slot}"] = SWA.n

    def emit_a1(self):
        CC, PE, DV, AC = self.CC, self.PE, self.DV, self.AC
        pb = self.pb
        wready = self.wa1_ready

        def emit_lnx_load(tc):
            jblk = tc // 2
            half = tc % 2
            par = tc % 2
            cc_w = (self.cc_lnxt0 if tc == 0 else
                    (self.cc_lnxt1 if tc == 1 else 0))
            pe_free = self.g(f"a1_lnxfree_{par}")
            SL0 = self.S[f"lnx{par}"]
            if par == 0:
                for q in range(4):
                    self.op("sync", lambda e, jblk=jblk, half=half,
                            cc_w=cc_w, q=q: e.dma_start(
                        out=self.lnx.ap()[:, q * 4096:(q + 1) * 4096].rearrange(
                            "p (a t) -> p a t", t=TC),
                        in_=self.lnxt_all[half][
                            (jblk * NIC + q * 8) * P:
                            (jblk * NIC + q * 8 + 8) * P, :].rearrange(
                            "(a p) t -> p a t", p=128)),
                        waits=([(PE, pe_free), (CC, cc_w)] if q == 0 else []),
                        inc=SL0, amt=16)
            else:
                for hic, buf in ((0, self.lnxb0), (8, self.lnxb0),
                                 (16, self.lnxb1), (24, self.lnxb1)):
                    boff = (hic % 16) * TC
                    self.op("sync", lambda e, jblk=jblk, half=half, cc_w=cc_w,
                            hic=hic, buf=buf, boff=boff: e.dma_start(
                        out=buf[:, boff:boff + 4096].rearrange(
                            "p (a t) -> p a t", t=TC),
                        in_=self.lnxt_all[half][
                            (jblk * NIC + hic) * P:(jblk * NIC + hic + 8) * P,
                            :].rearrange("(a p) t -> p a t", p=128)),
                        waits=([(PE, pe_free), (CC, cc_w)] if hic == 0 else []),
                        inc=SL0, amt=16)
            self.t[f"a1_lnx_ld_{par}"] = SL0.n
        emit_lnx_load(0)
        wo_pieces = [(oc, hh) for hh in range(4) for oc in range(8)]
        self.cc_woutq = [0] * 4
        for tc in range(NTC):
            if tc + 1 < NTC:
                emit_lnx_load(tc + 1)
            ld_cur = self.t[f"a1_lnx_ld_{tc % 2}"]
            for pi in (2 * tc, 2 * tc + 1):
                oc, hh = wo_pieces[pi]
                extra = ([(AC, self.g(f"wsA_free_{pi % 2}"))] if pi < 2 else ())
                self.emit_wout_piece(oc, hh, pi % 2, extra)
                if oc == 7:
                    self.op("gpsimd", lambda e, hh=hh: e.collective_compute(
                        "AllGather", ALU.bypass, replica_groups=RG,
                        ins=[self.wout_in[hh].opt()],
                        outs=[self.wout_all[hh].opt()]),
                        waits=[(self.S["wo0"], self.S["wo0"].n),
                               (self.S["wo1"], self.S["wo1"].n)], inc=self.CC)
                    self.cc_woutq[hh] = self.CC.n
            self.emit_wa2_piece(2 * tc)
            self.emit_wa2_piece(2 * tc + 1)
            par = tc % 2
            SL0 = self.S[f"lnx{par}"]
            ld0 = ld_cur
            t0 = tc * TC

            def rhs_ap(ic, par=par):
                if par == 0:
                    return self.lnx.ap()[:, ic * TC:(ic + 1) * TC]
                if ic < 16:
                    return self.lnxb0[:, ic * TC:(ic + 1) * TC]
                return self.lnxb1[:, (ic - 16) * TC:(ic - 15) * TC]
            for ot in range(9):
                bank = pb[ot % 2]
                m = 128 if ot < 8 else NH
                for ic in range(NIC):
                    w0 = ic * WA1C + ot * 128
                    w = [(AC, wready[ic])] if tc == 0 else []
                    if ic == 0:
                        w += [(SL0, ld0),
                              (DV, self.g(f"a1_dvfree_{ot % 2}")),
                              (AC, self.g(f"a1_acfree_{ot % 2}"))]
                    rap = rhs_ap(ic)
                    self.op("tensor", lambda e, w0=w0, m=m, bank=bank,
                            ic=ic, rap=rap: e.matmul(
                        out=bank.ap()[0:m, :],
                        lhsT=self.wbuf.ap()[:, w0:w0 + m],
                        rhs=rap,
                        start=(ic == 0), stop=(ic == NIC - 1)),
                        waits=w, inc=(PE if ic == NIC - 1 else None))
                pe_ot = PE.n
                if ot == 8:
                    SD = self.S["ds0"]
                    self.op("vector", lambda e, bank=bank: e.tensor_copy(
                        out=self.dtev[:, 0:TC], in_=bank.ap()[0:NH, :]),
                        waits=[(PE, pe_ot), (SD, self.g("a1_dtsp"))], inc=DV)
                    self.t[f"a1_dvfree_{ot % 2}"] = DV.n
                    dv_dt = DV.n
                    self.op("gpsimd", lambda e, t0=t0: e.dma_start(
                        out=self.dt_spill[:, t0:t0 + TC], in_=self.dtev[:, 0:TC]),
                        waits=[(DV, dv_dt)], inc=SD, amt=16)
                    self.t["a1_dtsp"] = SD.n
                else:
                    zslot = ot % 2
                    SZ = self.S[f"zs{zslot}"]
                    zev = self.zevac if zslot == 0 else self.zevac2
                    self.op("scalar", lambda e, bank=bank, zev=zev: e.activation(
                        out=zev[:, 0:TC], in_=bank.ap()[:], func=AF.Copy),
                        waits=[(PE, pe_ot), (SZ, self.g(f"a1_zsp_{zslot}"))],
                        inc=AC)
                    self.t[f"a1_acfree_{ot % 2}"] = AC.n
                    ac_z = AC.n
                    self.op("gpsimd", lambda e, ot=ot, t0=t0, zev=zev: e.dma_start(
                        out=self.z_spill[ot * 128:(ot + 1) * 128, t0:t0 + TC],
                        in_=zev[:, 0:TC]),
                        waits=[(AC, ac_z)], inc=SZ, amt=16)
                    self.t[f"a1_zsp_{zslot}"] = SZ.n
            self.t[f"a1_lnxfree_{par}"] = PE.n
        self.a1_done_pe = PE.n
        self.a1_spill_fin = [(self.S["zs0"], self.S["zs0"].n),
                             (self.S["zs1"], self.S["zs1"].n),
                             (self.S["ds0"], self.S["ds0"].n),
                             (self.S["ds1"], self.S["ds1"].n)]
        self.wo_fin = [(self.S["wo0"], self.S["wo0"].n),
                       (self.S["wo1"], self.S["wo1"].n)]

    def emit_a2(self):
        CC, PE, DV, AC = self.CC, self.PE, self.DV, self.AC
        SDBG = self.S["dbg"]
        wconv_gate = [(PE, self.a1_done_pe)]
        if self.debug:
            self.op("gpsimd", lambda e: e.dma_start(
                out=self.dbgout["d_wbuf"], in_=self.wbuf.ap()[:, 0:NIC * WA1C]),
                waits=[(PE, self.a1_done_pe)], inc=SDBG, amt=16)
            wconv_gate.append((SDBG, SDBG.n))
        SWL = self.S["wa2l"]
        wa2_cache_done = [(self.S["wa2"], self.S["wa2"].n),
                          (self.S["cm"], self.S["cm"].n)]
        self.t["a2_ipfree_0"] = self.g("a1_acfree_0")
        self.t["a2_ipfree_1"] = self.g("a1_acfree_1")
        self.a1_dtev_dv = self.g("a1_dvfree_0")
        for ic in range(NIC):
            w = list(wconv_gate) + wa2_cache_done if ic < 2 else []
            self.op("sync", lambda e, ic=ic: e.dma_start(
                out=self.wbuf.ap()[:, ic * WA2C:(ic + 1) * WA2C],
                in_=self.wa2bf[ic * 128:(ic + 1) * 128, :]),
                waits=w, inc=SWL, amt=16)
        self.a2_wready_wait = (SWL, SWL.n)
        self.op("vector", lambda e: e.memset(self.S_in.ap()[:], 0.0),
                waits=[(PE, self.a1_done_pe)], inc=DV)
        self.op("vector", lambda e: e.memset(self.combo.ap()[:], 0.0), inc=DV)
        self.op("vector", lambda e: e.memset(self.S_bf.ap()[:], 0.0), inc=DV)
        self.op("vector", lambda e: e.memset(
            self.xbcpre.rearrange("p (g c) -> p g c", g=10)[:, :, 0:3], 0.0), inc=DV)
        self.dv_a2init = DV.n
        self.emit_ip_lnx_load(0)
        for part in range(4):
            self.emit_ip_part(0, part)
        self.emit_conv(0)
        for tc in range(NTC):
            dv_acs, sp_combo = self.emit_dt_chain(tc)
            if tc + 1 < NTC:
                self.emit_ip_lnx_load(tc + 1)
            ac_conv = self.g("a2_conv_ac")
            for ch in range(NCHTC):
                self.emit_chunk(tc, ch, dv_acs, sp_combo, ac_conv)
            if tc + 1 < NTC:
                for pp in range(4):
                    self.emit_ip_part(tc + 1, pp)
                self.emit_conv(tc + 1)
            self.emit_a2_tail(tc)
        self.a2_done_pe = self.PE.n
        self.a2_lnx_last = self.g("a2_lnxfree")
        self.op("gpsimd", lambda e: e.collective_compute(
            "AllReduce", ALU.add, replica_groups=RG,
            ins=[self.ssq_in.opt()], outs=[self.ssq_out.opt()]),
            waits=[(self.S["sq"], self.S["sq"].n)], inc=CC)
        self.cc_ssq = CC.n
        self.cc_y = CC.n
        if self.debug:
            d = self.dbgout
            pairs = [("d_z", self.z_spill),
                     ("d_dt", self.dt_spill), ("d_y", self.y_blk),
                     ("d_ssq", self.ssq_out)]
            for name, src in pairs:
                self.op("gpsimd", lambda e, name=name, src=src: e.dma_start(
                    out=d[name], in_=src), waits=[(CC, self.cc_y)],
                    inc=SDBG, amt=16)
            for h in range(2):
                for rb in range(4):
                    self.op("gpsimd", lambda e, h=h, rb=rb: e.dma_start(
                        out=d["d_lnxt"][rb * 8192:(rb + 1) * 8192,
                                        h * TC:(h + 1) * TC],
                        in_=self.lnxt_all[h][rb * 8192:(rb + 1) * 8192, :]),
                        waits=[(CC, self.cc_y)], inc=SDBG, amt=16)

    def emit_ip_lnx_load(self, tc):
        jblk = tc // 2
        half = tc % 2
        pe_free = self.g("a2_lnxfree", self.a1_done_pe)
        SL = self.S["lnx0"]
        for q in range(4):
            self.op("sync", lambda e, jblk=jblk, half=half, q=q: e.dma_start(
                out=self.lnx.ap()[:, q * 4096:(q + 1) * 4096].rearrange(
                    "p (a t) -> p a t", t=TC),
                in_=self.lnxt_all[half][
                    (jblk * NIC + q * 8) * P:
                    (jblk * NIC + q * 8 + 8) * P, :].rearrange(
                    "(a p) t -> p a t", p=128)),
                waits=([(self.PE, pe_free)] if q == 0 else []),
                inc=SL, amt=16)
        self.t["a2_lnx_ld"] = SL.n

    def emit_ip_part(self, tc, part):
        PE, DV, AC = self.PE, self.DV, self.AC
        pb = self.pb
        ots = [(0, 3), (3, 6), (6, 8), (8, 10)][part]
        if part == 0:
            if tc > 0:
                self.op("vector", lambda e: e.tensor_copy(
                    out=self.xbcpre.rearrange("p (g c) -> p g c", g=10)[:, :, 0:3],
                    in_=self.xbcpre.rearrange("p (g c) -> p g c", g=10)[:, :, 512:515]),
                    waits=[(PE, self.g("a2_conv_pe"))], inc=DV)
                self.t["a2_hist_dv"] = DV.n
            else:
                self.t["a2_hist_dv"] = self.dv_a2init
        dv_hist = self.g("a2_hist_dv")
        SL0 = self.S["lnx0"]
        ld0 = self.g("a2_lnx_ld")
        for ot in range(*ots):
            bank = pb[ot % 2]
            for ic in range(NIC):
                w0 = ic * WA2C + ot * 128
                w = ([self.a2_wready_wait]
                     if (tc == 0 and part == 0 and ic == 0 and ot == 0) else [])
                if ic == 0:
                    w += [(SL0, ld0), (DV, self.dv_a2init),
                          (AC, self.g(f"a2_ipfree_{ot % 2}")),
                          (DV, max(dv_hist, self.a1_dtev_dv))]
                self.op("tensor", lambda e, w0=w0, bank=bank, ic=ic: e.matmul(
                    out=bank.ap()[:],
                    lhsT=self.wbuf.ap()[:, w0:w0 + 128],
                    rhs=self.lnx.ap()[:, ic * TC:(ic + 1) * TC],
                    start=(ic == 0), stop=(ic == NIC - 1)),
                    waits=w, inc=(PE if ic == NIC - 1 else None))
            pe_ot = PE.n
            self.op("scalar", lambda e, ot=ot, bank=bank: e.activation(
                out=self.xbcpre[:, ot * 515 + 3:ot * 515 + 3 + TC],
                in_=bank.ap()[:], func=AF.Copy),
                waits=[(PE, pe_ot)], inc=AC)
            self.t[f"a2_ipfree_{ot % 2}"] = AC.n
        if part == 3:
            self.t["a2_lnxfree"] = PE.n
            self.t["a2_ip_ac"] = AC.n

    def emit_conv(self, tc):
        PE, DV, AC = self.PE, self.DV, self.AC
        pb = self.pb
        ac_ip = self.g("a2_ip_ac")
        for g in range(10):
            bank = pb[2 + (g % 2)]
            for k in range(4):
                w = []
                if k == 0:
                    w = [(AC, ac_ip), (AC, self.g(f"a2_cvfree_{g % 2}")),
                         (DV, self.dv_diag),
                         (DV, self.g("a2_st_dv")),
                         (DV, self.g("a2_yo_free")),
                         (DV, self.g("a2_ctok_free"))]
                self.op("tensor", lambda e, g=g, k=k, bank=bank: e.matmul(
                    out=bank.ap()[:],
                    lhsT=self.dgw.ap()[:, (g * 4 + k) * P:(g * 4 + k + 1) * P],
                    rhs=self.xbcpre[:, g * 515 + k:g * 515 + k + TC],
                    start=(k == 0), stop=(k == 3)),
                    waits=(w if k == 0 else []),
                    inc=(PE if k == 3 else None))
            pe_cv = PE.n
            self.op("scalar", lambda e, g=g, bank=bank: e.activation(
                out=self.xbcpost[:, g * 512:(g + 1) * 512],
                in_=bank.ap()[:], func=AF.Silu, bias=self.cb[:, g:g + 1]),
                waits=[(PE, pe_cv)], inc=AC)
            self.t[f"a2_cvfree_{g % 2}"] = AC.n
        self.t["a2_conv_pe"] = PE.n
        self.t["a2_conv_ac"] = AC.n

    def emit_dt_chain(self, tc):
        PE, DV, AC = self.PE, self.DV, self.AC
        SDTR = self.S["dtr"]
        self.op("sync", lambda e, tc=tc: e.dma_start(
            out=self.dtr, in_=self.dt_spill[:, tc * TC:(tc + 1) * TC]),
            waits=self.a1_spill_fin + [(DV, self.g("a2_dt_free")),
                   (AC, self.g("a2_dtac")), (PE, self.g("a2_ssqmm_pe")),
                   (DV, self.g("a2_st_dv"))],
            inc=SDTR, amt=16)
        ld_dt = SDTR.n
        self.op("scalar", lambda e: e.activation(
            out=self.sp1, in_=self.dtr, func=AF.Abs,
            bias=self.dtbias[0:NH, :]),
            waits=[(SDTR, ld_dt), (DV, self.g("a2_yev_dv")),
                   (AC, AC.n)], inc=AC)
        self.t["a2_dtac"] = AC.n
        self.op("scalar", lambda e: e.activation(
            out=self.sp1, in_=self.sp1, func=AF.Exp, scale=-1.0),
            waits=[(AC, AC.n)], inc=AC)
        self.op("scalar", lambda e: e.activation(
            out=self.sp1, in_=self.sp1, func=AF.Ln, bias=1.0),
            waits=[(AC, AC.n)], inc=AC)
        ac_sp = AC.n
        self.t["a2_sp_ac"] = ac_sp
        self.op("vector", lambda e: e.tensor_scalar(
            out=self.sp2[:], in0=self.dtr, scalar1=self.dtbias[0:NH, :],
            scalar2=0.0, op0=ALU.add, op1=ALU.max), waits=[(SDTR, ld_dt)], inc=DV)
        self.op("vector", lambda e: e.tensor_tensor(
            out=self.combo.ap()[32:48, :], in0=self.sp1, in1=self.sp2[:],
            op=ALU.add),
            waits=[(AC, ac_sp), (PE, self.g("a2_ct_pe")), (DV, DV.n)], inc=DV)
        self.t["a2_dt_free"] = DV.n
        self.op("vector", lambda e: e.tensor_scalar(
            out=self.cumA, in0=self.combo.ap()[32:48, :],
            scalar1=self.apos[32:32 + NH, :], scalar2=None, op0=ALU.mult),
            waits=[(AC, self.ac_apos), (DV, DV.n)], inc=DV)
        self.op("vector", lambda e: e.tensor_tensor_scan(
            out=self.combo.ap()[0:NH, :], data0=self.scanmask.ap()[:],
            data1=self.cumA, initial=0.0, op0=ALU.mult, op1=ALU.add),
            waits=[(PE, self.g("a2_ct_pe")), (DV, DV.n)], inc=DV)
        dv_acs = DV.n
        SCM = self.S["chl"]
        self.op("vector", lambda e: e.tensor_copy(
            out=self.chi, in_=self.combo.ap()[0:NH, :]),
            waits=[(DV, dv_acs), (PE, self.g("a2_ytr_pe")),
                   (SCM, self.g("a2_chl_sp"))], inc=DV)
        self.op("vector", lambda e: e.tensor_tensor(
            out=self.clo, in0=self.combo.ap()[0:NH, :], in1=self.chi,
            op=ALU.subtract), waits=[(DV, DV.n)], inc=DV)
        dv_chl = DV.n
        self.op("gpsimd", lambda e: e.dma_start(
            out=self.combo_drb[0:NH, :], in_=self.chi),
            waits=[(DV, dv_chl), (self.S["fl"], self.g("a2_flat_sp"))],
            inc=SCM, amt=16)
        self.op("gpsimd", lambda e: e.dma_start(
            out=self.combo_drb[NH:2 * NH, :], in_=self.clo), inc=SCM, amt=16)
        self.t["a2_chl_sp"] = SCM.n
        for ch in range(NCHTC):
            w = []
            if ch == 0:
                w = [(DV, dv_acs), (DV, self.g("a2_ctok_free")),
                     (DV, self.g("a2_st_dv")), (AC, self.g("a2_conv_ac"))]
            self.op("tensor", lambda e, ch=ch: e.transpose(
                out=self.pb[3].ap()[:, ch * CH:(ch + 1) * CH],
                in_=self.combo.ap()[:, ch * CH:(ch + 1) * CH],
                identity=self.identf[:]),
                waits=w, inc=(PE if ch == NCHTC - 1 else None))
        pe_ct = PE.n
        self.t["a2_ct_pe"] = pe_ct
        self.op("vector", lambda e: e.tensor_copy(
            out=self.ctok.ap()[:], in_=self.pb[3].ap()[:]),
            waits=[(PE, pe_ct), (AC, self.g("a2_ctok_ac_war"))], inc=DV)
        self.t["a2_ctok_dvct"] = DV.n
        self.t["a2_ctok_free"] = DV.n
        return dv_acs, SCM.n

    def emit_a2_tail(self, tc):
        CC, DV = self.CC, self.DV
        self.op("gpsimd", lambda e, tc=tc: e.dma_start(
            out=self.ssq_in[:, tc * TC:(tc + 1) * TC], in_=self.ssqr.ap()[0:1, :]),
            waits=[(DV, self.g("a2_ssq_dv"))], inc=self.S["sq"], amt=16)
        self.t["a2_ssqr_sp"] = self.S["sq"].n
        self.op("gpsimd", lambda e, tc=tc: e.dma_start(
            out=self.y_blk[tc * DIC:(tc + 1) * DIC, :].rearrange(
                "(a p) t -> p a t", p=128),
            in_=self.yfm.rearrange("p (a t) -> p a t", a=8)),
            waits=[(DV, self.g("a2_gate_dv"))], inc=self.S["yi"], amt=16)
        self.t["a2_y_sp"] = self.S["yi"].n
        self.op("gpsimd", lambda e, tc=tc: e.collective_compute(
            "AllGather", ALU.bypass, replica_groups=RG,
            ins=[self.y_blk[tc * DIC:(tc + 1) * DIC, :].opt()],
            outs=[self.y_allb[tc * NCORES * DIC:(tc + 1) * NCORES * DIC, :].opt()]),
            waits=[(self.S["yi"], self.S["yi"].n)], inc=CC)

    def emit_chunk(self, tc, ch, dv_acs, sp_combo, ac_conv):
        CC, PE, DV, AC = self.CC, self.PE, self.DV, self.AC
        SFL, SCM = self.S["fl"], self.S["chl"]
        pb = self.pb
        xoff = ch * CH
        B_fm = self.xbcpost[:, 8 * 512 + xoff:8 * 512 + xoff + CH]
        C_fm = self.xbcpost[:, 9 * 512 + xoff:9 * 512 + xoff + CH]
        dv_ct = self.g("a2_ctok_dvct")
        negacs_cols = self.ctok.ap()[:, ch * CH:ch * CH + NH]
        dt_cols = self.ctok.ap()[:, ch * CH + 32:ch * CH + 32 + NH]
        self.op("gpsimd", lambda e, ch=ch: e.dma_start(
            out=self.fbf.rearrange("o (g t) -> o g t", g=2 * NH),
            in_=self.combo_drb[:, ch * CH:(ch + 1) * CH].unsqueeze(0)),
            waits=[(SCM, sp_combo), (PE, self.g("a2_flat_pe"))], inc=SFL, amt=16)
        sp_flat = SFL.n
        self.t["a2_flat_sp"] = sp_flat
        for t in range(8):
            self.op("tensor", lambda e, t=t, xoff=xoff: e.transpose(
                out=self.pb6.ap()[:, t * 128:(t + 1) * 128],
                in_=self.xbcpost[:, t * 512 + xoff:t * 512 + xoff + CH],
                identity=self.identb[:]),
                waits=([(AC, ac_conv), (DV, self.g("a2_xtr_free")),
                        (DV, self.g("a2_gate_dv"))] if t == 0 else []),
                inc=(PE if t == 7 else None))
        pe_xtr = PE.n
        self.t["a2_xtr_pe"] = pe_xtr
        dtb_ap = dt_cols.unsqueeze(2).broadcast_to([P, NH, HD])
        self.op("vector", lambda e, dtb_ap=dtb_ap: e.tensor_tensor(
            out=self.xdt.rearrange("p (h d) -> p h d", h=NH),
            in0=self.pb6.ap().rearrange("p (h d) -> p h d", h=NH),
            in1=dtb_ap, op=ALU.mult),
            waits=[(PE, pe_xtr), (DV, dv_ct), (PE, self.g("a2_yd_pe")),
                   (DV, self.g("a2_xdtd_dv"))], inc=DV)
        dv_xdt = DV.n
        dbc_ap = self.dbc.ap().unsqueeze(2).broadcast_to([P, NH, HD])
        self.op("vector", lambda e, dbc_ap=dbc_ap: e.tensor_tensor(
            out=self.xD.rearrange("p (h d) -> p h d", h=NH),
            in0=self.pb6.ap().rearrange("p (h d) -> p h d", h=NH),
            in1=dbc_ap, op=ALU.mult),
            waits=[(DV, self.dv_dbc), (DV, self.g("a2_yev_dv"))], inc=DV)
        self.t["a2_xtr_free"] = DV.n
        dv_xD = DV.n
        negT_hi = self.fbf[0:1, 0:2048].rearrange(
            "o (h t) -> o h t", t=CH)[:, :, 127:128].squeeze(2)
        negT_lo = self.fbf[0:1, 2048:4096].rearrange(
            "o (h t) -> o h t", t=CH)[:, :, 127:128].squeeze(2)
        self.op("tensor", lambda e, negT_hi=negT_hi: e.matmul(
            out=pb[4].ap()[:, 0:NH], lhsT=self.ones_bf_row(),
            rhs=negT_hi, start=True, stop=False),
            waits=[(SFL, sp_flat), (AC, self.g("a2_ab_free_0")),
                   (DV, self.g("a2_ssqev_dv"))], inc=None)
        self.op("tensor", lambda e, negT_lo=negT_lo: e.matmul(
            out=pb[4].ap()[:, 0:NH], lhsT=self.ones_bf_row(),
            rhs=negT_lo, start=False, stop=True), waits=[], inc=PE)
        pe_negT = PE.n
        self.op("vector", lambda e: e.tensor_tensor(
            out=self.dsdiff[:], in0=negacs_cols, in1=pb[4].ap()[:, 0:NH],
            op=ALU.subtract),
            waits=[(PE, pe_negT), (AC, self.g("a2_ds_ac")), (DV, dv_ct)], inc=DV)
        dv_dsd = DV.n
        self.op("scalar", lambda e: e.activation(
            out=self.expTb[:], in_=pb[4].ap()[:, 0:NH], func=AF.Exp, scale=-1.0),
            waits=[(PE, pe_negT), (DV, self.g("a2_scan_a_dv"))], inc=AC)
        ac_expT = AC.n
        self.op("scalar", lambda e: e.activation(
            out=self.dstok[:], in_=self.dsdiff[:], func=AF.Exp),
            waits=[(DV, dv_dsd), (DV, self.g("a2_xdtd_dv"))], inc=AC)
        ac_ds = AC.n
        self.t["a2_ds_ac"] = ac_ds
        self.op("scalar", lambda e, negacs_cols=negacs_cols: e.activation(
            out=self.atok[:], in_=negacs_cols, func=AF.Exp, scale=-1.0),
            waits=[(DV, dv_ct), (DV, self.g("a2_yev_dv"))], inc=AC)
        ac_atok = AC.n
        ds_ap = self.dstok.ap().unsqueeze(2).broadcast_to([P, NH, HD])
        self.op("vector", lambda e, ds_ap=ds_ap: e.tensor_tensor(
            out=self.xdtd.rearrange("p (h d) -> p h d", h=NH),
            in0=self.xdt.rearrange("p (h d) -> p h d", h=NH),
            in1=ds_ap, op=ALU.mult),
            waits=[(AC, ac_ds), (DV, dv_xdt), (PE, self.g("a2_st_pe"))], inc=DV)
        dv_xdtd = DV.n
        self.t["a2_xdtd_dv"] = dv_xdtd
        self.op("tensor", lambda e, B_fm=B_fm, C_fm=C_fm: e.matmul(
            out=pb[2].ap()[:, 0:128], lhsT=B_fm, rhs=C_fm, start=True, stop=True),
            waits=[(AC, ac_conv), (DV, self.g("a2_gm_dv")),
                   (DV, self.g("a2_st_dv"))], inc=PE)
        pe_g = PE.n
        self.t["a2_g_pe"] = pe_g
        self.op("vector", lambda e: e.tensor_tensor(
            out=self.Gm.ap()[:], in0=pb[2].ap()[:, 0:128], in1=self.trimask[:],
            op=ALU.mult), waits=[(PE, pe_g)], inc=DV)
        dv_gm = DV.n
        self.t["a2_gm_dv"] = dv_gm
        self.op("tensor", lambda e, B_fm=B_fm: e.transpose(
            out=self.pb6.ap()[:, 0:128], in_=B_fm, identity=self.identb[:]),
            waits=[(DV, self.g("a2_xtr_free")), (DV, self.g("a2_btok_dv"))],
            inc=PE)
        pe_bt = PE.n
        self.op("vector", lambda e: e.tensor_copy(
            out=self.Btok.ap()[:], in_=self.pb6.ap()[:, 0:128]),
            waits=[(PE, pe_bt)], inc=DV)
        dv_btok = DV.n
        self.t["a2_btok_dv"] = dv_btok
        for r in range(2):
            self.op("tensor", lambda e, C_fm=C_fm, r=r: e.matmul(
                out=pb[3].ap()[:], lhsT=C_fm,
                rhs=self.S_bf.ap()[:, r * 512:(r + 1) * 512],
                start=True, stop=True),
                waits=[(AC, ac_conv), (DV, self.g("a2_sbf_dv")),
                       (DV, self.g("a2_yo_free")), (DV, dv_ct)], inc=PE)
            pe_yo = PE.n
            self.t["a2_yo_pe"] = pe_yo
            for q in (2 * r, 2 * r + 1):
                abank = pb[4] if q % 2 == 0 else pb[5]
                w = [(SFL, sp_flat), (AC, self.g(f"a2_ab_free_{q % 2}"))]
                if q == 0:
                    w += [(DV, dv_dsd), (AC, ac_expT),
                          (DV, self.g("a2_ssqev_dv"))]
                self.op("tensor", lambda e, q=q, abank=abank: e.matmul(
                    out=abank.ap()[:], lhsT=self.ones_bf_row(),
                    rhs=self.fbf[0:1, q * 512:(q + 1) * 512],
                    start=True, stop=False), waits=w, inc=None)
                self.op("tensor", lambda e, q=q, abank=abank: e.matmul(
                    out=abank.ap()[:], lhsT=self.ones_bf_row(),
                    rhs=self.fbf[0:1, 2048 + q * 512:2048 + (q + 1) * 512],
                    start=False, stop=True), waits=[], inc=PE)
                pe_ab = PE.n
                if q == 3:
                    self.t["a2_flat_pe"] = PE.n
                for es in range(4):
                    h = q * 4 + es
                    self.op("scalar", lambda e, h=h, es=es, abank=abank,
                            negacs_cols=negacs_cols: e.activation(
                        out=self.Ebuf.ap()[:, es * P:(es + 1) * P],
                        in_=abank.ap()[:, es * 128:(es + 1) * 128],
                        func=AF.Exp, scale=-1.0, bias=negacs_cols[:, h:h + 1]),
                        waits=([(PE, pe_ab), (DV, self.g("a2_m_dv")),
                                (DV, dv_ct)] if es == 0
                               else [(AC, AC.n)]), inc=AC)
                self.t[f"a2_ab_free_{q % 2}"] = AC.n
                ac_e = AC.n
                self.op("vector", lambda e: e.scalar_tensor_tensor(
                    out=self.Mbuf.ap().rearrange("p (h l) -> p h l", h=4),
                    in0=self.Ebuf.ap().rearrange("p (h l) -> p h l", h=4),
                    scalar=1.0,
                    in1=self.Gm.ap().unsqueeze(1).broadcast_to([P, 4, P]),
                    op0=ALU.min, op1=ALU.mult),
                    waits=[(AC, ac_e), (DV, dv_gm),
                           (PE, self.g("a2_yd_pe"))], inc=DV)
                self.t["a2_m_dv"] = DV.n
                dv_m = DV.n
                for es in range(4):
                    h = q * 4 + es
                    hh = h - r * 8
                    w = [(DV, dv_m), (DV, dv_xdt)]
                    if hh == 0:
                        w.append((DV, self.g("a2_yd_free")))
                    self.op("tensor", lambda e, hh=hh, h=h, es=es: e.matmul(
                        out=self.pb7.ap()[:, hh * HD:(hh + 1) * HD],
                        lhsT=self.Mbuf.ap()[:, es * P:(es + 1) * P],
                        rhs=self.xdt[:, h * HD:(h + 1) * HD],
                        start=True, stop=True),
                        waits=(w if es == 0 else []),
                        inc=(PE if es == 3 else None))
                self.t["a2_yd_pe"] = PE.n
            pe_yd = PE.n
            a_ap = self.atok.ap()[:, r * 8:(r + 1) * 8].unsqueeze(2).broadcast_to(
                [P, 8, HD])
            self.op("vector", lambda e, a_ap=a_ap: e.tensor_tensor(
                out=self.yscr.ap().rearrange("p (h d) -> p h d", h=8),
                in0=pb[3].ap().rearrange("p (h d) -> p h d", h=8),
                in1=a_ap, op=ALU.mult),
                waits=[(PE, pe_yo), (AC, ac_atok),
                       (AC, self.g("a2_sp_ac"))], inc=DV)
            self.t["a2_yo_free"] = DV.n
            self.op("vector", lambda e: e.tensor_tensor(
                out=self.yscr.ap()[:], in0=self.pb7.ap()[:], in1=self.yscr.ap()[:],
                op=ALU.add), waits=[(PE, pe_yd), (DV, DV.n)], inc=DV)
            self.t["a2_yd_free"] = DV.n
            ytw = [(DV, dv_xD), (PE, self.g("a2_ytr_pe")), (DV, DV.n)]
            if ch == 0:
                ytw.append((SCM, self.g("a2_chl_sp")))
            self.op("vector", lambda e, r=r: e.tensor_tensor(
                out=self.ytok[:, r * 512:(r + 1) * 512],
                in0=self.yscr.ap()[:], in1=self.xD[:, r * 512:(r + 1) * 512],
                op=ALU.add),
                waits=ytw,
                inc=DV)
            self.t["a2_yev_dv"] = DV.n
        dv_yev = DV.n
        for r in range(2):
            stbank = pb[2] if r == 0 else pb[3]
            stw = [(DV, dv_btok), (DV, dv_xdtd)]
            if r == 0:
                stw.append((DV, dv_gm))
            else:
                stw.append((DV, self.g("a2_yo_free")))
            self.op("tensor", lambda e, r=r, stbank=stbank: e.matmul(
                out=stbank.ap()[:], lhsT=self.Btok.ap()[:],
                rhs=self.xdtd[:, r * 512:(r + 1) * 512],
                start=True, stop=True), waits=stw, inc=PE)
            self.t[f"a2_st_pe_{r}"] = PE.n
            self.t["a2_st_pe"] = PE.n
        expT_ap = self.expTb.ap().unsqueeze(2).broadcast_to([P, NH, HD])
        self.op("vector", lambda e, expT_ap=expT_ap: e.tensor_tensor(
            out=self.S_a.ap().rearrange("p (h d) -> p h d", h=NH),
            in0=self.S_in.ap().rearrange("p (h d) -> p h d", h=NH),
            in1=expT_ap, op=ALU.mult),
            waits=[(AC, ac_expT), (PE, self.g("a2_ssqmm_pe"))], inc=DV)
        self.t["a2_scan_a_dv"] = DV.n
        dv_sa = DV.n
        for r in range(2):
            stbank = pb[2] if r == 0 else pb[3]
            self.op("vector", lambda e, r=r, stbank=stbank: e.tensor_tensor(
                out=self.S_in.ap()[:, r * 512:(r + 1) * 512],
                in0=self.S_a.ap()[:, r * 512:(r + 1) * 512],
                in1=stbank.ap()[:], op=ALU.add),
                waits=[(PE, self.g(f"a2_st_pe_{r}")), (DV, dv_sa)], inc=DV)
        self.t["a2_st_dv"] = DV.n
        dv_sti = DV.n
        self.op("vector", lambda e: e.tensor_copy(
            out=self.S_bf.ap()[:], in_=self.S_in.ap()[:]),
            waits=[(PE, self.g("a2_yo_pe")), (DV, dv_sti)], inc=DV)
        self.t["a2_sbf_dv"] = DV.n
        for t in range(8):
            self.op("tensor", lambda e, t=t: e.transpose(
                out=self.pb6.ap()[:, t * 128:(t + 1) * 128],
                in_=self.ytok[:, t * 128:(t + 1) * 128],
                identity=self.identb[:]),
                waits=([(DV, dv_yev), (DV, dv_btok),
                        (DV, self.g("a2_gate_dv"))] if t == 0 else []),
                inc=(PE if t == 7 else None))
        pe_ytr = PE.n
        self.t["a2_ytr_pe"] = pe_ytr
        for grp in range(2):
            SZT = self.S[f"z{grp}"]
            zlw = [(AC, self.g("a2_zs_ac"))]
            if tc == 0 and ch == 0:
                zlw += self.a1_spill_fin
            self.op("sync", lambda e, grp=grp, tc=tc, ch=ch: e.dma_start(
                out=self.zstg.rearrange("p (a t) -> p a t", t=CH),
                in_=self.z_spill[grp * 512:(grp + 1) * 512,
                                 tc * TC + ch * CH:tc * TC + (ch + 1) * CH].rearrange(
                    "(a p) t -> p a t", p=128)),
                waits=zlw, inc=SZT, amt=16)
            ld_zt = SZT.n
            zw = [(SZT, ld_zt), (DV, self.g("a2_gate_dv"))]
            if tc == 0 and ch == 0 and grp == 0:
                zw += self.wo_fin
            self.op("scalar", lambda e, grp=grp, ch=ch: e.activation(
                out=self.siluz.rearrange("p (a t) -> p a t", t=512)[
                    :, grp * 4:(grp + 1) * 4, ch * CH:(ch + 1) * CH],
                in_=self.zstg.rearrange("p (a t) -> p a t", t=CH),
                func=AF.Silu), waits=zw, inc=AC)
            self.t["a2_zs_ac"] = AC.n
        ac_z = AC.n
        gw = [(PE, pe_ytr), (AC, ac_z), (self.S["yi"], self.g("a2_y_sp"))]
        if tc == 0 and ch == 0:
            gw += self.wo_fin
        self.op("vector", lambda e, ch=ch: e.tensor_tensor(
            out=self.yfm.rearrange("p (a t) -> p a t", a=8)[:, :, ch * CH:(ch + 1) * CH],
            in0=self.pb6.ap().rearrange("p (a t) -> p a t", a=8),
            in1=self.siluz.rearrange("p (a t) -> p a t", a=8)[:, :, ch * CH:(ch + 1) * CH],
            op=ALU.mult), waits=gw, inc=DV)
        self.t["a2_gate_dv"] = DV.n
        dv_gate = DV.n
        self.op("scalar", lambda e, ch=ch: e.activation(
            out=self.sqs_bf.rearrange("p (a t) -> p a t", a=8),
            in_=self.yfm.rearrange("p (a t) -> p a t", a=8)[:, :, ch * CH:(ch + 1) * CH],
            func=AF.Square),
            waits=[(DV, dv_gate), (PE, self.g("a2_ssqmm_pe")),
                   (DV, self.g("a2_st_dv"))], inc=AC)
        ac_sq = AC.n
        for t in range(8):
            w = []
            if t == 0:
                w = [(AC, ac_sq), (AC, self.g("a2_ab_free_0")),
                     (DV, self.g("a2_ssqev_dv"))]
            self.op("tensor", lambda e, t=t: e.matmul(
                out=pb[4].ap()[0:1, 0:128],
                lhsT=self.trimask.ap()[:, 127:128],
                rhs=self.sqs_bf[:, t * 128:(t + 1) * 128],
                start=(t == 0), stop=(t == 7)), waits=w,
                inc=(PE if t == 7 else None))
        pe_ssq = PE.n
        self.t["a2_ssqmm_pe"] = pe_ssq
        self.op("vector", lambda e, ch=ch: e.tensor_copy(
            out=self.ssqr.ap()[0:1, ch * CH:(ch + 1) * CH],
            in_=pb[4].ap()[0:1, 0:128]),
            waits=[(PE, pe_ssq), (self.S["sq"], self.g("a2_ssqr_sp"))], inc=DV)
        self.t["a2_ssqev_dv"] = DV.n
        self.t["a2_ssq_dv"] = DV.n
        self.t["a2_ctok_ac_war"] = self.AC.n
        if self.debug and tc == 0:
            SDBG = self.S["dbg"]
            d = self.dbgout
            for name, srcap in [("d_xdt", self.xdt), ("d_xdtd", self.xdtd),
                                ("d_ytok", self.ytok)]:
                self.op("gpsimd", lambda e, name=name, srcap=srcap, ch=ch: e.dma_start(
                    out=d[name][:, ch * 1024:(ch + 1) * 1024], in_=srcap),
                    waits=[(DV, self.DV.n)], inc=SDBG, amt=16)
            self.op("gpsimd", lambda e, ch=ch: e.dma_start(
                out=d["d_sbf"][:, ch * 1024:(ch + 1) * 1024], in_=self.S_bf[:]),
                waits=[(DV, self.DV.n)], inc=SDBG, amt=16)
            if ch == NCHTC - 1:
                self.op("gpsimd", lambda e: e.dma_start(
                    out=d["d_combo"], in_=self.combo[:]),
                    waits=[(DV, self.DV.n)], inc=SDBG, amt=16)
                self.op("gpsimd", lambda e: e.dma_start(
                    out=d["d_ctok"], in_=self.ctok[:]),
                    waits=[(DV, self.DV.n)], inc=SDBG, amt=16)
            self.t["a2_yev_dv"] = self.t["a2_yev_dv"]

    def emit_b(self):
        di = self.di
        CC, PE, DV, AC = self.CC, self.PE, self.DV, self.AC
        SBY, SOS, SDBG = self.S["by"], self.S["os"], self.S["dbg"]
        pb = self.pb
        tokv = {}
        def load_tok(e):
            with e.register("tok0r") as r, e.register("tok1r") as r2, \
                 e.register("tok2r") as r3:
                e.reg_load(r, di["tok0"][0:1, 0:1])
                tokv["v"] = e.snap(r, min_val=0, max_val=L - 1024)
                e.reg_mul(r2, tokv["v"], 16)
                tokv["q0"] = e.snap(r2, min_val=0, max_val=16 * (L - 1024))
                for qq in range(1, 4):
                    e.reg_add(r3, tokv["q0"], qq * 4096)
                    tokv[f"q{qq}"] = e.snap(
                        r3, min_val=qq * 4096,
                        max_val=16 * (L - 1024) + qq * 4096)
            return e.nop()
        self.ops["gpsimd"].append(lambda e: load_tok(e))
        self.b_tok = tokv

        def drive(g, n):
            for _ in range(n):
                try:
                    next(g)
                except StopIteration:
                    return False
            return True
        pregen = self.emit_c_pre_gen()

        def cj_all(js):
            for jj in js:
                yield from self.emit_c_j_gen(jj)
        cj03 = cj_all(range(4))
        for th in range(2):
            w = [(CC, self.cc_y), (PE, self.g("b_yfree"))]
            if th == 0:
                w += [(PE, self.a2_done_pe)]
            for qq in range(2):
                def ydma(e, th=th, qq=qq):
                    off = tokv[f"q{th * 2 + qq}"]
                    return e.dma_start(
                        out=self.b_y[:, qq * 16384:(qq + 1) * 16384].rearrange(
                            "p (a t) -> p a t", t=512),
                        in_=self.y_allb[bass.ds(off, 4096), :].rearrange(
                            "(a p) t -> p a t", p=128))
                self.op("gpsimd", ydma, waits=(w if qq == 0 else []),
                        inc=SBY, amt=16)
            sp_y = SBY.n
            for oc in range(32):
                slot = oc % 2
                wdst = self.b_w[:, slot * 8192:(slot + 1) * 8192]
                SBW = self.S[f"bw{slot}"]
                for qq in range(2):
                    self.op("sync", lambda e, oc=oc, wdst=wdst, qq=qq: e.dma_start(
                        out=wdst[:, qq * 4096:(qq + 1) * 4096].rearrange(
                            "p (a t) -> p a t", t=128),
                        in_=self.wout_all[oc // 8][
                            qq * 4096:(qq + 1) * 4096,
                            (oc % 8) * 128:(oc % 8 + 1) * 128].rearrange(
                            "(a p) t -> p a t", p=128)),
                        waits=([(CC, self.cc_woutq[oc // 8]),
                                (PE, self.g(f"b_wfree_{slot}")),
                                (PE, self.a2_lnx_last
                                 if (th == 0 and oc < 2) else 0)]
                               if qq == 0 else []),
                        inc=SBW, amt=16)
                ld_w = SBW.n
                mmbank = pb[oc % 2]
                for fc in range(64):
                    w = []
                    if fc == 0:
                        w = [(SBW, ld_w), (SBY, sp_y),
                             (DV, self.g(f"b_mmfree_{oc % 2}"))]
                    self.op("tensor", lambda e, fc=fc, wdst=wdst, mmbank=mmbank: e.matmul(
                        out=mmbank.ap()[:],
                        lhsT=wdst[:, fc * 128:(fc + 1) * 128],
                        rhs=self.b_y[:, fc * 512:(fc + 1) * 512],
                        start=(fc == 0), stop=(fc == 63)),
                        waits=w, inc=(PE if fc == 63 else None))
                pe_mm = PE.n
                self.t[f"b_wfree_{slot}"] = pe_mm
                bsb = self.b_sb2[oc % 2]
                self.op("vector", lambda e, mmbank=mmbank, bsb=bsb: e.tensor_copy(
                    out=bsb, in_=mmbank.ap()[:]),
                    waits=[(PE, pe_mm), (PE, self.g("b_trpe")),
                           (self.S["yi"], self.g("a2_y_sp") if th == 0 and oc == 0 else 0),
                           (DV, self.g(f"b_sbfree_{oc % 2}"))], inc=DV)
                dv_sb = DV.n
                self.t[f"b_mmfree_{oc % 2}"] = dv_sb
                self.t[f"b_sbfree_{oc % 2}"] = dv_sb
                trbank = pb[2 + (oc % 2)]
                for q in range(4):
                    self.op("tensor", lambda e, q=q, trbank=trbank,
                            bsb=bsb: e.transpose(
                        out=trbank.ap()[:, q * 128:(q + 1) * 128],
                        in_=bsb[:, q * 128:(q + 1) * 128],
                        identity=self.identf[:]),
                        waits=([(DV, dv_sb), (DV, self.g(f"b_ttsp_{oc % 2}"))]
                               if q == 0 else []),
                        inc=(PE if q == 3 else None))
                pe_tr = PE.n
                self.t["b_trpe"] = pe_tr
                btt = self.b_tt2[oc % 2]
                self.op("vector", lambda e, trbank=trbank, btt=btt: e.tensor_copy(
                    out=btt, in_=trbank.ap()[:]),
                    waits=[(PE, pe_tr), (SOS, self.g(f"b_ttdma_{oc % 2}"))],
                    inc=DV)
                dv_tt = DV.n
                self.t[f"b_ttsp_{oc % 2}"] = dv_tt
                self.op("gpsimd", lambda e, th=th, oc=oc, btt=btt: e.dma_start(
                    out=self.out_spill[th * 512:(th + 1) * 512,
                                       oc * 128:(oc + 1) * 128].rearrange(
                        "(q p) o -> p q o", p=128),
                    in_=btt.rearrange("p (q o) -> p q o", q=4)),
                    waits=[(DV, dv_tt)], inc=SOS, amt=16)
                self.t[f"b_ttdma_{oc % 2}"] = SOS.n
                if th == 0:
                    if oc >= 4:
                        drive(pregen, 1)
                else:
                    drive(cj03, 2)
            if th == 0:
                self.b_sp_th0 = SOS.n
                drive(pregen, 99)
            self.t["b_yfree"] = PE.n
        self.b_done_pe = PE.n
        self.b_done_sp = SOS.n
        drive(cj03, 999)
        if self.debug:
            self.op("gpsimd", lambda e: e.dma_start(
                out=self.dbgout["d_osp"], in_=self.out_spill),
                waits=[(SOS, self.b_done_sp)], inc=SDBG, amt=16)

    def emit_c_pre_gen(self):
        di = self.di
        CC, PE, DV, AC = self.CC, self.PE, self.DV, self.AC
        SCEB, SCS = self.S["ceb"], self.S["cs"]
        pb = self.pb
        tokv = self.b_tok
        dv_stg = 0
        for pc in range(4):
            w = [(PE, self.a2_done_pe)] if pc == 0 else [(DV, dv_stg)]
            self.op("sync", lambda e, pc=pc: e.dma_start(
                out=self.c_stg[0:64, :],
                in_=di["eb"][:, pc * 1024:(pc + 1) * 1024]),
                waits=w, inc=SCEB, amt=16)
            ld = SCEB.n
            self.op("vector", lambda e, pc=pc: e.tensor_copy(
                out=self.c_eb[0:64, pc * 1024:(pc + 1) * 1024],
                in_=self.c_stg[0:64, :]),
                waits=[(SCEB, ld)], inc=DV)
            dv_stg = DV.n
            if pc % 2 == 1:
                yield
        self.dv_eb = dv_stg
        for pc in range(2):
            self.op("sync", lambda e, pc=pc: e.dma_start(
                out=self.c_stg.rearrange("p (a c) -> p a c", a=16),
                in_=di["rwt"][pc * 2048:(pc + 1) * 2048, :].rearrange(
                    "(a p) c -> p a c", p=128)),
                waits=[(DV, dv_stg)], inc=SCEB, amt=16)
            ld = SCEB.n
            self.op("vector", lambda e, pc=pc: e.tensor_copy(
                out=self.c_rwt[:, pc * 1024:(pc + 1) * 1024],
                in_=self.c_stg[:, :]),
                waits=[(SCEB, ld)], inc=DV)
            dv_stg = DV.n
        self.dv_rwt = dv_stg
        yield
        self.op("sync", lambda e: e.dma_start(
            out=self.csc[0:1, 0:64], in_=di["rbias"]),
            waits=[(self.S["sq"], self.g("a2_ssqr_sp")), (DV, dv_stg)],
            inc=SCEB, amt=16)
        ld_rb = SCEB.n
        self.op("vector", lambda e: e.tensor_copy(
            out=self.rbias_b[:], in_=self.csc[0:1, 0:64]),
            waits=[(SCEB, ld_rb)], inc=DV)
        self.dv_rb = DV.n

        def ssq_dma(e):
            return e.dma_start(out=self.ssqrow8.ap()[0:1, :],
                               in_=self.ssq_out[0:1, bass.ds(tokv["v"], 1024)])
        self.op("gpsimd", ssq_dma, waits=[(CC, self.cc_ssq)], inc=SCS, amt=16)
        sp_ssq = SCS.n
        yield
        for jj in range(8):
            self.op("tensor", lambda e, jj=jj: e.matmul(
                out=pb[4].ap()[:, 64 + jj:65 + jj],
                lhsT=self.ssqrow8.ap()[0:1, jj * 128:(jj + 1) * 128],
                rhs=self.onesrow[0:1, 0:1],
                start=True, stop=True),
                waits=([(SCS, sp_ssq), (AC, self.g("a2_ab_free_0")),
                        (DV, self.g("a2_ssqev_dv"))] if jj == 0 else []),
                inc=(PE if jj == 7 else None))
        pe_ssqt = PE.n
        self.op("scalar", lambda e: e.activation(
            out=self.csc2[:, 0:8],
            in_=pb[4].ap()[:, 64:72], func=AF.Sqrt, scale=1.0 / 8192.0,
            bias=self.epsb[:]),
            waits=[(PE, pe_ssqt)], inc=AC)
        self.ac_sr = AC.n
        yield
        self.op("vector", lambda e: e.reciprocal(
            out=self.irms[:], in_=self.csc2[:, 0:8]),
            waits=[(AC, self.ac_sr)], inc=DV)
        self.dv_irms = DV.n

    def emit_c_j_gen(self, j):
        di = self.di
        CC, PE, DV, AC = self.CC, self.PE, self.DV, self.AC
        SCX, SOUT = self.S["cx"], self.S["out"]
        pb = self.pb
        bsp = self.b_sp_th0 if j < 4 else self.b_done_sp
        self.op("gpsimd", lambda e, j=j: e.dma_start(
            out=self.c_x, in_=di["x_own"][j * 128:(j + 1) * 128, :]),
            waits=[(self.S["os"], bsp), (SOUT, self.g("c_outsp")),
                   (DV, self.g("c_mixdone"))], inc=SCX, amt=16)
        self.op("gpsimd", lambda e, j=j: e.dma_start(
            out=self.c_out, in_=self.out_spill[j * 128:(j + 1) * 128, :]),
            inc=SCX, amt=16)
        ld_j = SCX.n
        self.op("vector", lambda e, j=j: e.tensor_scalar(
            out=self.c_t1, in0=self.c_x,
            scalar1=self.mcol[:, j:j + 1], scalar2=self.invcol[:, j:j + 1],
            op0=ALU.subtract, op1=ALU.mult),
            waits=[(SCX, ld_j), (DV, self.dv_irms), (DV, self.dv_rb),
                   (PE, self.g("c_lg_pe"))], inc=DV)
        self.op("vector", lambda e, j=j: e.scalar_tensor_tensor(
            out=self.c_out, in0=self.c_out, scalar=self.irms[:, j:j + 1],
            in1=self.c_x, op0=ALU.mult, op1=ALU.add), inc=DV)
        self.op("vector", lambda e: e.tensor_tensor(
            out=self.c_x2, in0=self.c_out, in1=self.c_t1, op=ALU.add),
            waits=[(DV, DV.n)], inc=DV)
        dv_x2 = DV.n
        yield
        self.op("scalar", lambda e: e.activation(
            out=self.c_dump, in_=self.c_x2, func=AF.Copy,
            accum_out=self.statsc[:, 0:1]),
            waits=[(DV, dv_x2), (DV, self.g("c_stats_done")),
                   (AC, AC.n)], inc=AC)
        self.op("scalar", lambda e: e.activation(
            out=self.c_dump, in_=self.c_x2, func=AF.Square,
            accum_out=self.statsc[:, 2:3]), waits=[(AC, AC.n)], inc=AC)
        ac_acc2 = AC.n
        self.op("vector", lambda e: e.tensor_scalar(
            out=self.statsc[:, 1:2], in0=self.statsc[:, 0:1],
            scalar1=1.0 / DM, scalar2=None, op0=ALU.mult),
            waits=[(AC, ac_acc2)], inc=DV)
        self.op("vector", lambda e: e.tensor_scalar(
            out=self.statsc[:, 3:4], in0=self.statsc[:, 2:3],
            scalar1=1.0 / DM, scalar2=None, op0=ALU.mult),
            waits=[(DV, DV.n)], inc=DV)
        self.op("vector", lambda e: e.tensor_tensor(
            out=self.statsc[:, 4:5], in0=self.statsc[:, 1:2],
            in1=self.statsc[:, 1:2], op=ALU.mult),
            waits=[(DV, DV.n)], inc=DV)
        self.op("vector", lambda e: e.tensor_tensor(
            out=self.statsc[:, 5:6], in0=self.statsc[:, 3:4],
            in1=self.statsc[:, 4:5], op=ALU.subtract),
            waits=[(DV, DV.n)], inc=DV)
        dv_v2 = DV.n
        self.t["c_stats_done"] = dv_v2
        self.op("scalar", lambda e: e.activation(
            out=self.statsc[:, 6:7], in_=self.statsc[:, 5:6],
            func=AF.Sqrt, bias=self.epsb[:]), waits=[(DV, dv_v2)], inc=AC)
        ac_sd2 = AC.n
        yield
        self.op("vector", lambda e: e.reciprocal(
            out=self.statsc[:, 7:8], in_=self.statsc[:, 6:7]),
            waits=[(AC, ac_sd2)], inc=DV)
        dv_i2 = DV.n
        self.op("vector", lambda e: e.tensor_scalar(
            out=self.c_ln2f, in0=self.c_x2,
            scalar1=self.statsc[:, 1:2], scalar2=self.statsc[:, 7:8],
            op0=ALU.subtract, op1=ALU.mult), waits=[(DV, dv_i2)], inc=DV)
        self.op("vector", lambda e: e.tensor_copy(
            out=self.c_ln2b[:], in_=self.c_ln2f),
            waits=[(DV, DV.n), (AC, ac_acc2)], inc=DV)
        dv_ln2 = DV.n
        yield
        for half in range(2):
            for b in range(4 * half, 4 * half + 4):
                hh = b % 2
                for t in range(4):
                    fc = b * 4 + t
                    self.op("tensor", lambda e, fc=fc, t=t, hh=hh: e.transpose(
                        out=self.pb6.ap()[:, hh * 512 + t * 128:
                                          hh * 512 + (t + 1) * 128],
                        in_=self.c_ln2b[:, fc * 128:(fc + 1) * 128],
                        identity=self.identb[:]),
                        waits=([(DV, dv_ln2), (DV, self.g("c_trfree_0")),
                                (DV, self.g("c_trfree_1"))]
                               if t == 0 else []),
                        inc=(PE if t == 3 else None))
                pe_tr = PE.n
                self.op("vector", lambda e, b=b, hh=hh: e.tensor_copy(
                    out=self.c_ln2t[:, b * 512:(b + 1) * 512],
                    in_=self.pb6.ap()[:, hh * 512:(hh + 1) * 512]),
                    waits=[(PE, pe_tr)], inc=DV)
                self.t[f"c_trfree_{hh}"] = DV.n
            yield
        dv_lt = DV.n
        for ic in range(NIC):
            w = []
            if ic == 0:
                w = [(DV, dv_lt), (DV, self.g("c_lgfree"))]
                if j == 0:
                    w.append((AC, self.ac_sr))
            self.op("tensor", lambda e, ic=ic: e.matmul(
                out=pb[4].ap()[:, 0:64],
                lhsT=self.c_ln2t[:, ic * 128:(ic + 1) * 128],
                rhs=self.c_rwt[:, ic * 64:(ic + 1) * 64],
                start=(ic == 0), stop=False), waits=w)
        self.op("tensor", lambda e: e.matmul(
            out=pb[4].ap()[:, 0:64], lhsT=self.ones_bf_row(),
            rhs=self.rbias_b[:], start=False, stop=True), inc=PE)
        pe_lg = PE.n
        self.t["c_lg_pe"] = pe_lg
        CS = self.csc
        self.op("vector", lambda e: e.tensor_copy(out=CS[:, 0:64],
                                                  in_=pb[4].ap()[:, 0:64]),
                waits=[(PE, pe_lg)], inc=DV)
        self.op("vector", lambda e: e.tensor_copy(out=CS[:, 64:128],
                                                  in_=CS[:, 0:64]), inc=DV)
        dv_lg = DV.n
        self.t["c_lgfree"] = dv_lg
        self.op("vector", lambda e: e.reduce_max(
            out=self.csc2[:, 0:1], in_=CS[:, 64:128], axis=AXX,
            negate=True), inc=DV)
        yield
        for it in range(7):
            self.op("vector", lambda e: e.reduce_max(
                out=self.wsum[:], in_=CS[:, 64:128], axis=AXX), inc=DV)
            dv_rm = DV.n
            self.op("vector", lambda e: e.tensor_scalar(
                out=CS[:, 128:192], in0=CS[:, 64:128],
                scalar1=self.wsum[:], scalar2=None, op0=ALU.is_equal),
                waits=[(DV, dv_rm)], inc=DV)
            dv_eq = DV.n
            self.op("vector", lambda e: e.scalar_tensor_tensor(
                out=CS[:, 64:128], in0=CS[:, 128:192], scalar=NEG,
                in1=CS[:, 64:128], op0=ALU.mult, op1=ALU.add),
                waits=[(DV, dv_eq)], inc=DV)
            if it == 3:
                yield
        self.op("vector", lambda e: e.reduce_max(
            out=self.csc2[:, 1:2], in_=CS[:, 64:128], axis=AXX), inc=DV)
        dv_t8 = DV.n
        yield
        self.op("vector", lambda e: e.tensor_scalar(
            out=CS[:, 128:192], in0=CS[:, 0:64],
            scalar1=self.csc2[:, 1:2], scalar2=None, op0=ALU.is_ge),
            waits=[(DV, dv_t8)], inc=DV)
        self.op("scalar", lambda e: e.activation(
            out=CS[:, 192:256], in_=CS[:, 0:64], func=AF.Exp,
            bias=self.csc2[:, 0:1]), waits=[(DV, dv_t8)], inc=AC)
        ac_e = AC.n
        self.op("vector", lambda e: e.reduce_sum(
            out=self.csc2[:, 2:3], in_=CS[:, 192:256], axis=AXX),
            waits=[(AC, ac_e)], inc=DV)
        dv_den = DV.n
        self.op("vector", lambda e: e.tensor_tensor(
            out=CS[:, 256:320], in0=CS[:, 192:256], in1=CS[:, 128:192],
            op=ALU.mult), waits=[(DV, DV.n)], inc=DV)
        self.op("vector", lambda e: e.reduce_sum(
            out=self.csc2[:, 3:4], in_=CS[:, 256:320], axis=AXX), inc=DV)
        dv_sk = DV.n
        self.op("vector", lambda e: e.reciprocal(
            out=self.csc2[:, 4:5], in_=self.csc2[:, 2:3]),
            waits=[(DV, dv_den)], inc=DV)
        dv_rd = DV.n
        self.op("vector", lambda e: e.tensor_tensor(
            out=self.wsum[:], in0=self.csc2[:, 3:4], in1=self.csc2[:, 4:5],
            op=ALU.mult), waits=[(DV, dv_sk), (DV, dv_rd)], inc=DV)
        dv_ws = DV.n
        self.op("vector", lambda e: e.tensor_scalar(
            out=CS[:, 320:384], in0=CS[:, 256:320],
            scalar1=self.csc2[:, 4:5], scalar2=None, op0=ALU.mult),
            waits=[(DV, dv_rd)], inc=DV)
        dv_up = DV.n
        self.op("vector", lambda e: e.tensor_copy(
            out=self.ubf[:], in_=CS[:, 320:384]),
            waits=[(DV, dv_up), (DV, dv_ws)], inc=DV)
        dv_u = DV.n
        yield
        self.op("tensor", lambda e: e.transpose(
            out=self.pb6.ap()[0:64, 0:128], in_=self.ubf[:],
            identity=self.identb[:]),
            waits=[(DV, dv_u), (DV, self.g("c_trfree_0")),
                   (DV, self.g("c_trfree_1"))], inc=PE)
        pe_ut = PE.n
        self.op("vector", lambda e: e.tensor_copy(
            out=self.ubT[:], in_=self.pb6.ap()[0:64, 0:128]),
            waits=[(PE, pe_ut)], inc=DV)
        dv_ut = DV.n
        self.t["c_trfree_0"] = dv_ut
        for ob in range(8):
            bmbank = pb[5] if ob % 2 == 0 else self.pb7
            w = [(DV, dv_ut), (DV, self.dv_eb),
                 (DV, self.g(f"c_bmfree_{ob % 2}"))]
            if j == 0 and ob < 2:
                w += [(AC, self.g("a2_ab_free_1")), (DV, self.g("a2_yev_dv"))]
            self.op("tensor", lambda e, ob=ob, bmbank=bmbank: e.matmul(
                out=bmbank.ap()[:], lhsT=self.ubT[:],
                rhs=self.c_eb[0:64, ob * 512:(ob + 1) * 512],
                start=True, stop=True),
                waits=w, inc=PE)
            pe_bm = PE.n
            self.op("vector", lambda e, ob=ob, bmbank=bmbank: e.scalar_tensor_tensor(
                out=self.c_t1[:, ob * 512:(ob + 1) * 512],
                in0=self.c_ln2f[:, ob * 512:(ob + 1) * 512],
                scalar=self.wsum[:], in1=bmbank.ap()[:],
                op0=ALU.mult, op1=ALU.add),
                waits=[(PE, pe_bm), (DV, dv_ws)], inc=DV)
            self.t[f"c_bmfree_{ob % 2}"] = DV.n
            self.op("vector", lambda e, ob=ob: e.tensor_tensor(
                out=self.c_outst[:, ob * 512:(ob + 1) * 512],
                in0=self.c_t1[:, ob * 512:(ob + 1) * 512],
                in1=self.c_x2[:, ob * 512:(ob + 1) * 512], op=ALU.add),
                waits=[(DV, DV.n)], inc=DV)
            if ob == 3:
                yield
        dv_o = DV.n
        self.t["c_mixdone"] = dv_o
        self.op("gpsimd", lambda e, j=j: e.dma_start(
            out=self.out[j * 128:(j + 1) * 128, :], in_=self.c_outst),
            waits=[(DV, dv_o)], inc=SOUT, amt=16)
        self.t["c_outsp"] = SOUT.n
        yield

    def emit_c_rest(self):
        for j in range(4, 8):
            for _ in self.emit_c_j_gen(j):
                pass
        final = self.S["out"].n
        self.ops["gpsimd"].append(
            lambda e, final=final: e.wait_ge(self.S["out"].h, final))

    def ones_bf_row(self):
        return self.trimask.ap()[0:1, :]


_CACHE = {}


def _build(debug=False):
    key = ("nc", debug)
    if key not in _CACHE:
        _CACHE[key] = KB(debug=debug).build()
    return _CACHE[key]


def _rep3(v):
    out = np.zeros((P, 1), np.float32)
    out[0:NH, 0] = v
    out[32:32 + NH, 0] = v
    out[64:64 + NH, 0] = v
    return out


def _stage(inputs, k):
    f32 = np.float32
    bf = ml_dtypes.bfloat16
    x = np.asarray(inputs["L0_attn_hidden_states"], f32).reshape(L, DM)
    W = np.asarray(inputs["L0_attn_in_proj_w"], f32)
    cw = np.asarray(inputs["L0_attn_conv_w"], f32)
    cb = np.asarray(inputs["L0_attn_conv_b"], f32)
    dtb = np.asarray(inputs["L0_attn_dt_bias"], f32)
    alog = np.asarray(inputs["L0_attn_A_log"], f32)
    dv = np.asarray(inputs["L0_attn_D"], f32)
    nw = np.asarray(inputs["L0_attn_norm_w"], f32)
    wo = np.asarray(inputs["L0_attn_out_proj_w"], f32)
    rw = np.asarray(inputs["L0_ffn_router_weight"], f32)
    rb = np.asarray(inputs["L0_ffn_router_bias"], f32)
    eb = np.asarray(inputs["L0_ffn_expert_bias"], f32)

    zrows = W[1024 * k:1024 * (k + 1)]
    dtrows = W[18432 + 16 * k:18432 + 16 * (k + 1)]
    xrows = W[8192 + 1024 * k:8192 + 1024 * (k + 1)]
    brows = W[16384 + 128 * k:16384 + 128 * (k + 1)]
    crows = W[17408 + 128 * k:17408 + 128 * (k + 1)]
    wa1 = np.concatenate([zrows, dtrows], axis=0)
    wa2 = np.concatenate([xrows, brows, crows], axis=0)
    cwS = np.concatenate([cw[1024 * k:1024 * (k + 1)],
                          cw[8192 + 128 * k:8192 + 128 * (k + 1)],
                          cw[9216 + 128 * k:9216 + 128 * (k + 1)]], axis=0)
    cbS = np.concatenate([cb[1024 * k:1024 * (k + 1)],
                          cb[8192 + 128 * k:8192 + 128 * (k + 1)],
                          cb[9216 + 128 * k:9216 + 128 * (k + 1)]], axis=0)
    d = {
        "x_own": np.ascontiguousarray(x[1024 * k:1024 * (k + 1)]),
        "wa1t": np.ascontiguousarray(wa1.T),
        "wa2t": np.ascontiguousarray(wa2.T),
        "wot": np.ascontiguousarray(wo[:, 1024 * k:1024 * (k + 1)].T),
        "cw_pg": np.ascontiguousarray(
            cwS.reshape(10, 128, 4).transpose(1, 0, 2).reshape(128, 40)),
        "cb_pg": np.ascontiguousarray(cbS.reshape(10, 128).T),
        "dtb": _rep3(dtb[16 * k:16 * (k + 1)]),
        "alog": _rep3(alog[16 * k:16 * (k + 1)]),
        "dvec_row": dv[16 * k:16 * (k + 1)].reshape(1, NH),
        "normw_pg": np.ascontiguousarray(
            nw[1024 * k:1024 * (k + 1)].reshape(8, 128).T),
        "rwt": np.ascontiguousarray(rw.T),
        "rbias": rb.reshape(1, 64),
        "eb": eb,
        "identf": np.eye(P, dtype=f32),
        "identb": np.eye(P, dtype=f32).astype(bf),
        "onesrow_f": np.ones((1, P), f32),
        "onescol_f": np.ones((P, 1), f32),
        "trimask": np.triu(np.ones((P, P), f32)).astype(bf),
        "tok0": np.array([[1024 * k]], dtype=np.int32),
    }
    for key in d:
        if d[key].dtype == np.float64:
            d[key] = d[key].astype(f32)
    return d


def kernel(**inputs):
    nc = _build(debug=False)
    in_maps = [_stage(inputs, k) for k in range(NCORES)]
    res = run_bass_kernel_spmd(nc, in_maps, core_ids=list(range(NCORES)))
    out = np.concatenate([np.asarray(res.results[k]["out"], np.float32)
                          for k in range(NCORES)], axis=0)
    return out.reshape(1, L, DM)
